# version 02
# baseline (speedup 1.0000x reference)
import math
import numpy as np
import concourse.bass as bass
import concourse.mybir as mybir
from concourse.bass_utils import run_bass_kernel_spmd

F32 = mybir.dt.float32
BF16 = mybir.dt.bfloat16
U32 = mybir.dt.uint32
AF = mybir.ActivationFunctionType
ALU = mybir.AluOpType

NEG = -30000.0


class Cfg:
    def __init__(self, T=16384, debug=False):
        assert T % 2048 == 0
        self.T = T
        self.TT = 256
        self.C = 128
        self.CPT = self.TT // self.C
        self.T4 = T // 4
        self.T16 = T // 16
        self.NT1 = T // self.TT
        self.TPQ = self.NT1 // 4
        self.P2TT = min(512, self.T16)
        self.NT2 = (4 * self.T16) // self.P2TT
        self.KCH = 16
        self.NCOL = 3200
        self.NOC = 25
        self.debug = debug


class Sem:
    __slots__ = ("name", "n", "h")
    def __init__(self, name):
        self.name, self.n, self.h = name, 0, None


class Sched:
    ENGINES = ("sync", "gpsimd", "tensor", "vector", "scalar")

    def __init__(self):
        self.ops = []
        self.sems = {}

    def sem(self, name):
        s = self.sems.get(name)
        if s is None:
            s = self.sems[name] = Sem(name)
        return s

    def emit(self, eng, fn, wait=(), inc=()):
        ii = []
        for (s, d) in inc:
            s.n += d
            ii.append((s, d))
        self.ops.append((eng, fn, [w for w in wait if w is not None], ii))

    def replay(self, block):
        last = {e: {} for e in self.ENGINES}

        def mk(name):
            def run(eng):
                lw = last[name]
                for (e, fn, waits, incs) in self.ops:
                    if e != name:
                        continue
                    for (s, v) in waits:
                        if lw.get(s.name, -1) >= v:
                            continue
                        eng.wait_ge(s.h, v)
                        lw[s.name] = v
                    inst = fn(eng)
                    for (s, d) in incs:
                        inst.then_inc(s.h, d)
            return run

        block.sync(mk("sync"))
        block.gpsimd(mk("gpsimd"))
        block.tensor(mk("tensor"))
        block.vector(mk("vector"))
        block.scalar(mk("scalar"))


def _hadamard(n):
    H = np.array([[1.0]], np.float32)
    while H.shape[0] < n:
        H = np.block([[H, H], [H, -H]])
    return H


def shard_inputs(cfg, inp):
    T = cfg.T
    x = np.asarray(inp["L0_attn_x"], np.float32)[:, :T, :]
    Wqkv = np.asarray(inp["L0_attn_W_qkv"], np.float32)
    Wz = np.asarray(inp["L0_attn_W_z"], np.float32)
    Wb = np.asarray(inp["L0_attn_W_b"], np.float32)
    Wa = np.asarray(inp["L0_attn_W_a"], np.float32)
    convw = np.asarray(inp["L0_attn_conv_w"], np.float32)
    dtb = np.asarray(inp["L0_attn_dt_bias"], np.float32)
    alog = np.asarray(inp["L0_attn_A_log"], np.float32)
    normw = np.asarray(inp["L0_attn_norm_w"], np.float32)
    Wout = np.asarray(inp["L0_attn_W_out"], np.float32)
    W1 = np.asarray(inp["L0_ffn_W1"], np.float32)
    W2 = np.asarray(inp["L0_ffn_W2"], np.float32)
    W3 = np.asarray(inp["L0_ffn_W3"], np.float32)

    i1 = np.arange(128)
    bd64s = (i1[:, None] // 64 == i1[None, :] // 64).astype(np.float32)
    ejmat = np.zeros((128, 1024), np.float32)
    for j in range(8):
        ejmat[j, j * 128:(j + 1) * 128] = 1.0
    cf = np.concatenate([
        np.eye(128, dtype=np.float32),
        bd64s,
        -(1.0 - bd64s),
        np.where(i1[:, None] > i1[None, :], 0.0, NEG),
        np.where(i1[:, None] < i1[None, :], 0.0, NEG),
        np.where(i1[:, None] <= i1[None, :], 0.0, NEG),
        _hadamard(128) * (2.0 ** -7),
        ejmat,
    ], axis=1).astype(np.float32)

    smalls = np.zeros((128, 24), np.float32)
    smalls[0:8, 0:8] = np.eye(8)
    smalls[:, 11] = math.sqrt(2.0)
    smalls[:, 12] = 1e-6
    smalls[:, 13] = 1e-24
    smalls[:, 14] = -0.5
    smalls[:, 15] = 1e-5
    smalls[:, 16] = 1.0 / 2048.0
    smalls[:, 17] = 2.0
    smalls[:, 18] = -math.log(128.0)
    smalls[:, 19] = 1.0
    smalls[:, 20] = 1.0 / 128.0
    smalls[:, 21] = -1.0

    h64 = _hadamard(64)
    sel8 = np.zeros((1, 32), np.float32)
    for h in range(4):
        sel8[0, h * 8 + 2 * h] = -0.5
        sel8[0, h * 8 + 2 * h + 1] = -0.5

    in_maps = []
    for c in range(8):
        b, g = c // 4, c % 4
        js = np.arange(8 * g, 8 * g + 8)
        hs = np.arange(4 * g, 4 * g + 4)
        qcols = (hs[:, None] * 128 + np.arange(128)).reshape(-1)
        vcols = (js[:, None] * 128 + np.arange(128)).reshape(-1)
        bg = np.zeros((128, 2048), np.float32)
        bg[0:8] = Wb[js]
        bg[32:40] = Wa[js]
        Wbig = np.concatenate([Wqkv[qcols], Wqkv[2048 + qcols],
                               Wqkv[4096 + vcols], Wz[vcols], bg], axis=0)
        wbig = np.ascontiguousarray(
            Wbig.T.reshape(16, 128, cfg.NCOL).transpose(1, 0, 2)
                 .reshape(128, 16 * cfg.NCOL), np.float32)

        cw = np.zeros((128, 64), np.float32)
        qkvcols = np.concatenate([qcols, 2048 + qcols, 4096 + vcols])
        for kc in range(16):
            cw[:, kc * 4:(kc + 1) * 4] = convw[qkvcols[kc * 128:(kc + 1) * 128]]

        xt = np.ascontiguousarray(
            x[b].T.reshape(16, 128, T).transpose(1, 0, 2).reshape(128, 16 * T),
            np.float32)
        xq = np.ascontiguousarray(
            xt.reshape(128, 16, T)[:, :, g * cfg.T4:(g + 1) * cfg.T4]
              .reshape(128, 16 * cfg.T4))
        p2tok = np.concatenate([np.arange(q * cfg.T4 + g * cfg.T16,
                                          q * cfg.T4 + (g + 1) * cfg.T16)
                                for q in range(4)])
        xp2 = np.ascontiguousarray(
            xt.reshape(128, 16, T)[:, :, p2tok].reshape(128, 16 * 4 * cfg.T16))

        woutT = Wout.T.reshape(32, 128, 16, 128).transpose(1, 2, 0, 3) \
                      .reshape(128, 16 * 32 * 128)
        w13 = np.concatenate([W1, W3], axis=0)
        w13T = w13.T.reshape(16, 128, 128, 128).transpose(1, 2, 0, 3) \
                    .reshape(128, 128 * 16 * 128)
        sm = smalls.copy()
        sm[0:8, 8] = dtb[js]
        sm[0:8, 9] = alog[js]
        sm[0:128, 10] = normw

        in_maps.append({
            "xt": xt, "xq": xq, "xp2": xp2, "wbig": wbig, "convw": cw,
            "cf": cf, "h64": np.ascontiguousarray(h64, np.float32), "smalls": sm,
            "sel8": np.ascontiguousarray(sel8, np.float32),
            "woutT": np.ascontiguousarray(woutT, np.float32),
            "w13T": np.ascontiguousarray(w13T, np.float32),
            "w2T": np.ascontiguousarray(W2.T, np.float32),
            "rk": np.array([[g]], np.uint32),
        })
    return in_maps


def build(cfg):
    nc = bass.Bass("TRN2", target_bir_lowering=False, num_devices=8,
                   detect_race_conditions=False)
    sc = Sched()
    T, TT, C, CPT = cfg.T, cfg.TT, cfg.C, cfg.CPT
    T4, T16, P2TT = cfg.T4, cfg.T16, cfg.P2TT
    KCH, NCOL, NOC = cfg.KCH, cfg.NCOL, cfg.NOC

    DI = lambda n, s, dt=F32: nc.dram_tensor(n, s, dt, kind="ExternalInput").ap()
    d_xt = DI("xt", [128, KCH * T])
    d_xq = DI("xq", [128, KCH * T4])
    d_xp2 = DI("xp2", [128, KCH * 4 * T16])
    d_wbig = DI("wbig", [128, KCH * NCOL])
    d_convw = DI("convw", [128, 64])
    d_cf = DI("cf", [128, 7 * 128 + 1024])
    d_h64 = DI("h64", [64, 64])
    d_smalls = DI("smalls", [128, 24])
    d_sel8 = DI("sel8", [1, 32])
    d_woutT = DI("woutT", [128, 16 * 32 * 128])
    d_w13T = DI("w13T", [128, 128 * 16 * 128])
    d_w2T = DI("w2T", [8192, 2048])
    d_rk = DI("rk", [1, 1], U32)

    d_out = nc.dram_tensor("yout", [128, KCH * 4 * T16], F32, kind="ExternalOutput").ap()
    dbg = {}
    if cfg.debug:
        dbg["qkvc"] = nc.dram_tensor("dbg_qkvc", [128, 16 * T], BF16, kind="ExternalOutput").ap()
        dbg["gout"] = nc.dram_tensor("dbg_gout", [128, 8 * T], BF16, kind="ExternalOutput").ap()

    d_sti = nc.dram_tensor("stats_in", [1, 2 * T4], BF16, kind="Internal").ap()
    d_stg = nc.dram_tensor("stats_g", [1, 8 * T4], BF16, kind="Internal").ap()
    d_gq = [nc.dram_tensor(f"gq{q}", [1, 1024 * T4], BF16, kind="Internal").ap() for q in range(4)]
    d_gg = [nc.dram_tensor(f"gg{q}", [1, 4096 * T4], BF16, kind="Internal").ap() for q in range(4)]
    d_gsl = [nc.dram_tensor(f"gsl{q}", [1, 4096 * T16], BF16, kind="Internal").ap() for q in range(4)]
    d_w2t = nc.dram_tensor("w2tilde", [128, 16 * 64 * 128], BF16, kind="Internal").ap()
    d_wobf = nc.dram_tensor("woutbf", [128, 16 * 32 * 128], BF16, kind="Internal").ap()
    d_w13bf = nc.dram_tensor("w13bf", [128, 128 * 16 * 128], BF16, kind="Internal").ap()

    SD = sc.sem("sd"); GD = sc.sem("gd"); S2 = sc.sem("s2")
    TE = sc.sem("te"); DV = sc.sem("dv"); SE = sc.sem("se"); GC = sc.sem("gc")
    ST = sc.sem("st"); CCs = sc.sem("cc")
    LD1 = sc.sem("ld1")
    G0s = sc.sem("g0s")
    GX = [sc.sem("gx0"), sc.sem("gx1")]
    MR = [sc.sem("mr0"), sc.sem("mr1")]
    SB1 = sc.sem("sb1")
    SS = sc.sem("ss")
    WPl = sc.sem("wpl")
    WPs = sc.sem("wps")
    RS = [sc.sem(f"rs{q}") for q in range(4)]
    G2S = sc.sem("g2s")
    WOS = [sc.sem("wos0"), sc.sem("wos1")]
    W13S = [sc.sem(f"w13s{i}") for i in range(4)]
    W2TS = [sc.sem("w2ts0"), sc.sem("w2ts1")]
    X2S = [sc.sem("x2s0"), sc.sem("x2s1")]

    A = lambda n, s, dt: nc.alloc_sbuf_tensor(n, s, dt)
    p1g = []
    def A1(n, s, dt):
        g = nc.sbuf_tensor(n, s, dt)
        h = g.__enter__()
        p1g.append(g)
        return h
    c_ident = A("c_ident", [128, 128], BF16)
    c_h128 = A("c_h128", [128, 128], BF16)
    c_small = A("c_small", [128, 24], F32)
    c_ones_col = A("c_ones_col", [128, 1], BF16)
    c_ones_rowF = A("c_ones_rowF", [1, 128], F32)
    c_ones_row = A("c_ones_row", [1, 128], BF16)
    b_rkrg = A("b_rkrg", [1, 1], U32)
    c_bd64 = A1("c_bd64", [128, 128], BF16)
    c_offn = A1("c_offn", [128, 128], BF16)
    c_addLow = A1("c_addLow", [128, 128], F32)
    c_addUpS = A1("c_addUpS", [128, 128], F32)
    c_addUpI = A1("c_addUpI", [128, 128], F32)
    c_h64 = A1("c_h64", [64, 64], BF16)
    c_ej = A1("c_ej", [8, 1024], F32)
    c_ejb = A1("c_ejb", [8, 1024], BF16)
    c_ejcol = A1("c_ejcol", [8, 8], F32)
    c_sel8 = A1("c_sel8", [1, 32], F32)
    c_expAn = A1("c_expAn", [8, 1], F32)
    c_negW1 = A1("c_negW1", [1, NOC * 128], BF16)
    b_cw = A1("b_cw", [128, 64], F32)

    p_acc = [nc.alloc_psum_tensor("p_acc0", [128, 512], F32),
             nc.alloc_psum_tensor("p_acc1", [128, 512], F32)]
    p_kq = nc.alloc_psum_tensor("p_kq", [128, 512], F32)
    p_ap = nc.alloc_psum_tensor("p_ap", [128, 512], F32)
    p_sq = nc.alloc_psum_tensor("p_sq", [128, 512], F32)
    p_sq2 = nc.alloc_psum_tensor("p_sq2", [128, 512], F32)
    p_uso = nc.alloc_psum_tensor("p_uso", [128, 512], F32)
    p_bf = nc.alloc_psum_tensor("p_bf", [128, 1024], BF16)

    cons = {}

    def MM(region, out, lhsT, rhs, start=True, stop=True, wait=(), inc=()):
        w = list(wait)
        if start and region in cons:
            w.append(cons[region])
        sc.emit("tensor", lambda t, o=out, a=lhsT, b=rhs, s1=start, s2=stop:
                t.matmul(o, a, b, start=s1, stop=s2), wait=w, inc=inc)

    def TRP(region, out, in_, wait=(), inc=()):
        w = list(wait)
        if region in cons:
            w.append(cons[region])
        sc.emit("tensor", lambda t, o=out, a=in_: t.transpose(o, a, c_ident[:]),
                wait=w, inc=inc)

    def tmark():
        sc.emit("tensor", lambda t: t.nop(), inc=[(TE, 1)])
        return (TE, TE.n)

    def consume(region, mk):
        cons[region] = mk

    def EV(fn, wait=()):
        sc.emit("vector", fn, wait=wait, inc=[(DV, 1)])
        return (DV, DV.n)

    def ES(fn, wait=()):
        sc.emit("scalar", fn, wait=wait, inc=[(SE, 1)])
        return (SE, SE.n)

    def EG(fn, wait=()):
        sc.emit("gpsimd", fn, wait=wait, inc=[(GC, 1)])
        return (GC, GC.n)

    def TEi(fn, wait=()):
        sc.emit("tensor", fn, wait=wait, inc=[(TE, 1)])
        return (TE, TE.n)

    sc.emit("sync", lambda s: s.dma_start(out=c_small[:], in_=d_smalls), inc=[(SD, 16)])
    sc.emit("sync", lambda s: s.dma_start(out=c_ejcol[:], in_=d_smalls[0:8, 0:8]), inc=[(SD, 16)])
    sc.emit("sync", lambda s: s.dma_start(out=c_sel8[:], in_=d_sel8), inc=[(SD, 16)])
    sc.emit("sync", lambda s: s.dma_start(out=b_cw[:], in_=d_convw), inc=[(SD, 16)])
    sc.emit("sync", lambda s: s.dma_start(out=b_rkrg[:], in_=d_rk), inc=[(SD, 16)])
    rkldv = (SD, SD.n)
    sc.emit("sync", lambda s: s.dma_start(out=c_addLow[:], in_=d_cf[:, 384:512]), inc=[(SD, 16)])
    sc.emit("sync", lambda s: s.dma_start(out=c_addUpS[:], in_=d_cf[:, 512:640]), inc=[(SD, 16)])
    sc.emit("sync", lambda s: s.dma_start(out=c_addUpI[:], in_=d_cf[:, 640:768]), inc=[(SD, 16)])
    sc.emit("sync", lambda s: s.dma_start(out=c_ej[:], in_=d_cf[0:8, 896:1920]), inc=[(SD, 16)])
    for dst, src in [(c_ident, d_cf[:, 0:128]), (c_bd64, d_cf[:, 128:256]),
                     (c_offn, d_cf[:, 256:384]), (c_h128, d_cf[:, 768:896]),
                     (c_h64, d_h64), (c_ejb, d_cf[0:8, 896:1920])]:
        sc.emit("gpsimd", lambda g, d=dst, s=src: g.dma_start(out=d[:], in_=s), inc=[(LD1, 16)])
    sc.emit("gpsimd", lambda g: g.dma_start(out=d_wobf, in_=d_woutT), inc=[(LD1, 16)])
    sc.emit("gpsimd", lambda g: g.dma_start(out=d_w13bf, in_=d_w13T), inc=[(LD1, 16)])
    sc.emit("gpsimd", lambda g: g.memset(c_ones_col[:], 1.0))
    sc.emit("gpsimd", lambda g: g.memset(c_ones_row[:], 1.0))
    sc.emit("gpsimd", lambda g: g.memset(c_ones_rowF[:], 1.0), inc=[(GC, 1)])
    ld0 = [(SD, SD.n), (LD1, LD1.n), (GC, GC.n)]
    rcell = {}

    def _ldrk(s):
        rk = s.value_load(b_rkrg[0:1, 0:1])
        for m in range(4):
            rcell[m] = s.snap(rk + 4 * m)
        return None
    sc.emit("sync", _ldrk, wait=[rkldv])
    sA = ES(lambda s: s.activation(c_expAn[:], c_small[0:8, 9:10], AF.Exp), wait=ld0)
    EV(lambda v: v.tensor_scalar(c_expAn[:], c_expAn[:], -1.0, None, ALU.mult), wait=[sA])

    b_w = A1("b_w", [128, KCH * NCOL], BF16)
    b_x = [A1(f"b_x{i}", [128, KCH * TT], BF16) for i in range(2)]
    b_qkvc = A1("b_qkvc", [128, 16 * TT], BF16)
    b_zsil = [A1(f"b_zsil{i}", [128, 8 * TT], BF16) for i in range(2)]
    b_raw = [A1(f"b_raw{i}", [128, TT + 3], F32) for i in range(2)]
    b_zscr = A1("b_zscr", [128, TT], F32)
    b_zscr2 = A1("b_zscr2", [128, TT], F32)
    b_tail = A1("b_tail", [128, 16 * 3], F32)
    b_sq = A1("b_sq", [128, TT], BF16)
    b_mrow = [A1(f"b_mrow{i}", [1, TT], BF16) for i in range(2)]
    b_rstdb = [A1(f"b_rstdb{i}", [128, TT], F32) for i in range(2)]
    b_gst = A1("b_gst", [128, 8 * TT], BF16)
    ROWS = ["lnb", "lg", "t0", "L", "Lrel", "colA", "rowA", "rowO", "sk3a", "lendf"]
    b_rows = A1("b_rows", [8, len(ROWS) * TT], F32)
    RO = {n: i * TT for i, n in enumerate(ROWS)}
    b_sqr = A1("b_sqr", [8, TT], BF16)
    b_carry = A1("b_carry", [8, 1], F32)
    b_lnrp = A1("b_lnrp", [1, 8 * TT], F32)
    b_lnrkJ = A1("b_lnrkJ", [8, TT], F32)
    b_lnrqJ = A1("b_lnrqJ", [8, TT], F32)
    b_S = [A1(f"b_S{j}", [128, 128], F32) for j in range(8)]
    b_Sbf = [A1(f"b_Sbf{j}", [128, 128], BF16) for j in range(8)]
    Wk = {}
    for n in ["kk", "kqt", "ktm", "vtm", "expa", "expat", "expot", "kkbd", "kkoff",
              "abd", "atbd", "etneg", "mot", "k3", "qh",
              "n2", "n2t", "n4", "n4t", "n8", "n8t", "n16", "n16t", "n32", "n32t",
              "cg1", "cg2", "u", "wt", "osq", "sigbc"]:
        Wk[n] = A1("s_" + n, [128, 128], BF16)
    b_arg = A1("s_arg", [128, 128], F32)
    b_R = A1("s_R", [128, 256], BF16)
    b_Y0 = A1("s_Y0", [128, 256], BF16)
    b_Z = A1("s_Z", [128, 256], BF16)
    b_X = A1("s_X", [128, 256], BF16)
    b_scols = A1("s_cols", [128, 5], F32)
    b_scolsE = A1("s_colsE", [128, 5], F32)
    b_sr1 = A1("s_sr1", [1, 128], F32)
    b_sbfr = A1("s_sbfr", [1, 128], BF16)
    b_g1 = A1("s_g1", [128, 128], F32)
    b_strow = A1("b_strow", [1, 2 * TT], F32)
    b_stbf = A1("b_stbf", [1, 2 * TT], BF16)
    assert 8 * TT >= 2048
    b_w2pre = b_gst[0:64, 0:2048]
    b_w2ps = b_gst[64:128, 0:2048]

    sc.emit("gpsimd", lambda g: g.memset(b_tail[:], 0.0))
    sc.emit("gpsimd", lambda g: g.memset(b_carry[:], 0.0))
    for j in range(8):
        sc.emit("gpsimd", lambda g, j=j: g.memset(b_S[j][:], 0.0))
        sc.emit("gpsimd", lambda g, j=j: g.memset(b_Sbf[j][:], 0.0))
    sc.emit("gpsimd", lambda g: g.nop(), inc=[(GC, 1)])
    zinit = (GC, GC.n)

    sc.emit("gpsimd", lambda g: g.dma_start(out=b_w[:], in_=d_wbig), inc=[(LD1, 16)])
    wv1 = (LD1, LD1.n)
    for oc in range(NOC):
        for kc in range(KCH):
            MM("kq", p_kq[0:1, (oc % 4) * 128:(oc % 4) * 128 + 128],
               c_ones_col[:], b_w[:, kc * NCOL + oc * 128: kc * NCOL + oc * 128 + 128],
               start=(kc == 0), stop=(kc == KCH - 1),
               wait=([wv1] if (oc == 0 and kc == 0) else ()),
               inc=([(TE, 1)] if kc == KCH - 1 else ()))
        if oc % 4 == 3 or oc == NOC - 1:
            tev = (TE, TE.n)
            lo = (oc // 4) * 4
            v = EV(lambda v, lo=lo, oc=oc: v.tensor_scalar(
                c_negW1[0:1, lo * 128:(oc + 1) * 128],
                p_kq[0:1, 0:(oc + 1 - lo) * 128], -1.0, None, ALU.mult), wait=[tev])
            consume("kq", v)

    w2T_v = d_w2T.rearrange("(a b) n -> b a n", b=128)
    w2t_wv = d_w2t.rearrange("p (a n) -> p a n", a=64)
    for bb in range(128):
        sc.emit("gpsimd", lambda g, bb=bb: g.dma_start(out=b_w2pre, in_=w2T_v[bb]),
                wait=[cons.get("w2pre")], inc=[(WPl, 16)])
        gdv = (WPl, WPl.n)
        for nn in range(4):
            MM("ap", p_ap[0:64, 0:512], c_h64[:], b_w2pre[:, nn * 512:(nn + 1) * 512],
               wait=([gdv, cons.get("w2ps")] if nn == 0 else [cons.get("w2ps")]),
               inc=[(TE, 1)])
            tev = (TE, TE.n)
            v = EV(lambda v, nn=nn: v.tensor_scalar(
                b_w2ps[:, nn * 512:(nn + 1) * 512], p_ap[0:64, 0:512],
                c_small[0:64, 11:12], None, ALU.mult), wait=[tev])
            consume("ap", v)
        cons["w2pre"] = (TE, TE.n)
        dvv = (DV, DV.n)
        sc.emit("sync", lambda s, bb=bb: s.dma_start(
            out=d_w2t[bb:bb + 1, :].rearrange("p (oc a n) -> (p a) oc n", oc=16, a=64),
            in_=b_w2ps.rearrange("a (oc n) -> a oc n", oc=16)),
                wait=[dvv], inc=[(WPs, 16)])
        cons["w2ps"] = (WPs, WPs.n)
    w2prep_done = (WPs, WPs.n)
    cons["gst"] = w2prep_done

    NTQ = T4 // TT
    xq_v = d_xq.rearrange("p (k t) -> p k t", k=KCH)
    for t in range(NTQ):
        sc.emit("gpsimd", lambda g, t=t: g.dma_start(
            out=b_x[0][:].rearrange("p (k t) -> p k t", k=KCH),
            in_=xq_v[:, :, t * TT:(t + 1) * TT]),
            wait=[cons.get("x0b"), (ld0[1] if t == 0 else None)], inc=[(G0s, 16)])
        ldv = (G0s, G0s.n)
        for kc in range(KCH):
            MM("kq", p_kq[0:1, 0:TT], c_ones_col[:], b_x[0][:, kc * TT:(kc + 1) * TT],
               start=(kc == 0), stop=(kc == KCH - 1),
               wait=([ldv] if kc == 0 else ()), inc=([(TE, 1)] if kc == KCH - 1 else ()))
        for kc in range(KCH):
            sev = ES(lambda s, kc=kc: s.activation(
                b_sq[:, 0:TT], b_x[0][:, kc * TT:(kc + 1) * TT], AF.Square),
                wait=([ldv] if kc == 0 else [(TE, TE.n)]))
            MM("ap", p_ap[0:1, 0:TT], c_ones_col[:], b_sq[:, 0:TT],
               start=(kc == 0), stop=(kc == KCH - 1), wait=[sev], inc=[(TE, 1)])
        cons["x0b"] = (TE, TE.n)
        t1 = (TE, TE.n)
        v1 = EV(lambda v: v.tensor_scalar(b_strow[0:1, 0:TT], p_kq[0:1, 0:TT],
                                          c_small[0:1, 16:17], None, ALU.mult), wait=[t1])
        consume("kq", v1)
        v2 = EV(lambda v: v.scalar_tensor_tensor(
            b_strow[0:1, TT:2 * TT], b_strow[0:1, 0:TT], -1.0,
            b_strow[0:1, 0:TT], ALU.mult, ALU.mult), wait=[v1])
        v3 = EV(lambda v: v.scalar_tensor_tensor(
            b_strow[0:1, TT:2 * TT], p_ap[0:1, 0:TT], 1.0 / 2048.0,
            b_strow[0:1, TT:2 * TT], ALU.mult, ALU.add), wait=[v2])
        consume("ap", v3)
        s1 = ES(lambda s: s.activation(b_strow[0:1, TT:2 * TT], b_strow[0:1, TT:2 * TT],
                                       AF.Ln, bias=c_small[0:1, 15:16]), wait=[v3])
        s2 = ES(lambda s: s.activation(b_stbf[0:1, TT:2 * TT], b_strow[0:1, TT:2 * TT],
                                       AF.Exp, scale=c_small[0:1, 14:15]), wait=[s1])
        s3 = ES(lambda s: s.activation(b_stbf[0:1, 0:TT], b_strow[0:1, 0:TT], AF.Copy),
                wait=[v3])
        sc.emit("sync", lambda s, t=t: s.dma_start(
            out=d_sti[0:1, t * TT:(t + 1) * TT], in_=b_stbf[0:1, 0:TT]),
            wait=[s3], inc=[(SS, 16)])
        sc.emit("sync", lambda s, t=t: s.dma_start(
            out=d_sti[0:1, T4 + t * TT:T4 + (t + 1) * TT], in_=b_stbf[0:1, TT:2 * TT]),
            wait=[s2], inc=[(SS, 16)])
        cons["stbf"] = (SS, SS.n)
        cons["strow"] = (SS, SS.n)
    stv = (SS, SS.n)
    p0_x_done = (TE, TE.n)
    sc.emit("gpsimd", lambda g: g.collective_compute(
        "AllGather", ALU.bypass, replica_groups=[[0, 1, 2, 3], [4, 5, 6, 7]],
        ins=[d_sti], outs=[d_stg]), wait=[stv], inc=[(CCs, 1)])
    agst = (CCs, CCs.n)

    xt_v = d_xt.rearrange("p (k t) -> p k t", k=KCH)

    state = {"prev_head_te": None, "Scons": [zinit] * 8,
             "rows_done": zinit, "sqv": zinit}

    def scan_head(t, par, cc_, cs, h, jj, ksl, qsl, h_done):
        pht = state["prev_head_te"]
        blocks = ["rowA", "colA", "lnb", "sk3a", "lendf"]
        for i, blk in enumerate(blocks):
            MM("kq", p_kq[:, 256 + i:257 + i],
               b_rows[0:8, RO[blk] + cs:RO[blk] + cs + C],
               c_ejcol[0:8, jj:jj + 1],
               wait=([state["rows_done"]] if i == 0 else ()),
               inc=([(TE, 1)] if i == len(blocks) - 1 else ()))
        tev = (TE, TE.n)
        ecols = EV(lambda v: v.tensor_copy(b_scols[:], p_kq[:, 256:261]),
                   wait=[tev, pht])
        consume("kq", ecols)
        sexp = ES(lambda s: s.activation(b_scolsE[:], b_scols[:], AF.Exp), wait=[ecols])
        r3 = b_rows[0:8, :].rearrange("p (b t) -> p b t", t=TT)[:, 5:8, cs:cs + C]
        MM("uso", p_uso[:, 0:384], c_ej[0:8, jj * 128:(jj + 1) * 128], r3,
           wait=[state["rows_done"]], inc=[(TE, 1)])
        bcv = (TE, TE.n)
        a1 = EV(lambda v: v.scalar_tensor_tensor(
            b_arg[:], p_uso[:, 0:128], b_scols[:, 0:1], c_addLow[:], ALU.add, ALU.add),
            wait=[bcv, sexp])
        x1 = ES(lambda s: s.activation(Wk["expa"][:], b_arg[:], AF.Exp), wait=[a1])
        a2 = EV(lambda v: v.scalar_tensor_tensor(
            b_arg[:], p_uso[:, 128:256], b_scols[:, 1:2], c_addUpS[:], ALU.add, ALU.add),
            wait=[x1])
        x2 = ES(lambda s: s.activation(Wk["expat"][:], b_arg[:], AF.Exp), wait=[a2])
        a3 = EV(lambda v: v.scalar_tensor_tensor(
            b_arg[:], p_uso[:, 256:384], b_scols[:, 1:2], c_addUpI[:], ALU.add, ALU.add),
            wait=[x2])
        x3 = ES(lambda s: s.activation(Wk["expot"][:], b_arg[:], AF.Exp), wait=[a3])
        consume("uso", a3)
        m1 = EV(lambda v: v.tensor_tensor(Wk["abd"][:], Wk["expa"][:], Wk["kkbd"][:],
                                          ALU.mult), wait=[x1] + list(h_done[3:4]))
        m2 = EV(lambda v: v.tensor_tensor(Wk["atbd"][:], Wk["expat"][:], Wk["kkbd"][:],
                                          ALU.mult), wait=[x2])
        m3 = EV(lambda v: v.tensor_tensor(Wk["etneg"][:], Wk["expat"][:], Wk["kkoff"][:],
                                          ALU.mult), wait=[x2] + list(h_done[4:5]))
        m4 = EV(lambda v: v.tensor_tensor(Wk["mot"][:], Wk["expot"][:], Wk["kqt"][:],
                                          ALU.mult), wait=[x3] + list(h_done[2:3]))
        vsl = b_qkvc[:, (8 + jj) * TT + cs:(8 + jj) * TT + cs + C]
        TRP("bf", p_bf[:, 128:256], vsl, wait=[cons.get(f"qkvc{8 + jj}")], inc=[(TE, 1)])
        tev = (TE, TE.n)
        ev = EV(lambda v: v.tensor_copy(Wk["vtm"][:], p_bf[:, 128:256]), wait=[tev])
        consume("bf", ev)
        r1 = EV(lambda v: v.tensor_scalar(b_R[:, 0:128], Wk["ktm"][:],
                                          b_scolsE[:, 0:1], None, ALU.mult),
                wait=[sexp] + list(h_done[0:1]))
        r2 = EV(lambda v: v.tensor_scalar(b_R[:, 128:256], Wk["vtm"][:],
                                          b_scolsE[:, 2:3], None, ALU.mult), wait=[ev])
        r3v = EV(lambda v: v.tensor_scalar(Wk["k3"][:], Wk["ktm"][:],
                                           b_scolsE[:, 3:4], None, ALU.mult))
        MM("uso", p_uso[:, 384:512], c_ejb[0:8, jj * 128:(jj + 1) * 128],
           b_sqr[0:8, cs:cs + C], wait=[state["sqv"]], inc=[(TE, 1)])
        tev = (TE, TE.n)
        qv = EV(lambda v: v.tensor_tensor(Wk["qh"][:], qsl, p_uso[:, 384:512],
                                          ALU.mult), wait=[tev, cons.get(f"qkvc{h}")])
        consume("uso", qv)
        pw = [("n2", "n2t", "abd", "atbd"), ("n4", "n4t", "n2t", "n2"),
              ("n8", "n8t", "n4t", "n4"), ("n16", "n16t", "n8t", "n8"),
              ("n32", "n32t", "n16t", "n16")]
        mwait = [m1, m2]
        for (pa_, pb_, la, lb) in pw:
            MM("sq", p_sq[:, 0:128], Wk[la][:], Wk[lb][:], wait=mwait, inc=[(TE, 1)])
            t_a = (TE, TE.n)
            MM("sq2", p_sq2[:, 0:128], Wk[lb][:], Wk[la][:], inc=[(TE, 1)])
            t_b = (TE, TE.n)
            e_a = EV(lambda v, pa_=pa_: v.tensor_copy(Wk[pa_][:], p_sq[:, 0:128]),
                     wait=[t_a])
            e_b = EV(lambda v, pb_=pb_: v.tensor_copy(Wk[pb_][:], p_sq2[:, 0:128]),
                     wait=[t_b])
            consume("sq", e_a)
            consume("sq2", e_b)
            mwait = [e_a, e_b]
        cgA, cgB = "cg1", "cg2"
        ch = EV(lambda v, cgA=cgA: v.scalar_tensor_tensor(
            Wk[cgA][:], Wk["atbd"][:], -1.0, c_ident[:], ALU.mult, ALU.add),
                wait=[m2])
        for Pt in ("n2t", "n4t", "n8t", "n16t", "n32t"):
            MM("ap", p_ap[:, 0:128], Wk[Pt][:], Wk[cgA][:], wait=[ch], inc=[(TE, 1)])
            tev = (TE, TE.n)
            ch = EV(lambda v, cgA=cgA, cgB=cgB: v.tensor_tensor(
                Wk[cgB][:], p_ap[:, 0:128], Wk[cgA][:], ALU.add), wait=[tev])
            consume("ap", ch)
            cgA, cgB = cgB, cgA
        Tf = cgA
        MM("ap", p_ap[:, 0:256], Wk[Tf][:], b_R[:], wait=[ch, r1, r2], inc=[(TE, 1)])
        tev = (TE, TE.n)
        ey = EV(lambda v: v.tensor_copy(b_Y0[:], p_ap[:, 0:256]), wait=[tev])
        consume("ap", ey)
        MM("ap", p_ap[:, 0:256], Wk["etneg"][:], b_Y0[:], wait=[m3, ey], inc=[(TE, 1)])
        tev = (TE, TE.n)
        ez = EV(lambda v: v.tensor_tensor(b_Z[:], p_ap[:, 0:256], b_R[:], ALU.add),
                wait=[tev])
        consume("ap", ez)
        MM("ap", p_ap[:, 0:256], Wk[Tf][:], b_Z[:], wait=[ez], inc=[(TE, 1)])
        tev = (TE, TE.n)
        ex = EV(lambda v: v.tensor_copy(b_X[:], p_ap[:, 0:256]), wait=[tev])
        consume("ap", ex)
        TRP("bf", p_bf[:, 256:384], b_X[:, 0:128], wait=[ex], inc=[(TE, 1)])
        tev = (TE, TE.n)
        ew = EV(lambda v: v.tensor_copy(Wk["wt"][:], p_bf[:, 256:384]), wait=[tev])
        consume("bf", ew)
        MM("sq", p_sq[:, 0:128], Wk["wt"][:], b_Sbf[jj][:],
           wait=[ew, state["Scons"][jj]], inc=[(TE, 1)])
        tev = (TE, TE.n)
        eu = EV(lambda v: v.scalar_tensor_tensor(Wk["u"][:], p_sq[:, 0:128], -1.0,
                                                 b_X[:, 128:256], ALU.mult, ALU.add),
                wait=[tev])
        consume("sq", eu)
        MM("uso", p_uso[:, 0:128], b_Sbf[jj][:], Wk["qh"][:], stop=False,
           wait=[qv], inc=())
        MM("uso", p_uso[:, 0:128], Wk["u"][:], Wk["mot"][:], start=False,
           wait=[eu, m4], inc=[(TE, 1)])
        tO = (TE, TE.n)
        MM("sq2", p_sq2[:, 0:128], Wk["k3"][:], Wk["u"][:], wait=[r3v], inc=[(TE, 1)])
        tev = (TE, TE.n)
        es1 = EV(lambda v: v.scalar_tensor_tensor(
            b_S[jj][:], b_S[jj][:], b_scolsE[:, 4:5], p_sq2[:, 0:128],
            ALU.mult, ALU.add), wait=[tev])
        consume("sq2", es1)
        es2 = EV(lambda v: v.tensor_copy(b_Sbf[jj][:], b_S[jj][:]), wait=[es1])
        state["Scons"][jj] = es2
        so = ES(lambda s: s.activation(Wk["osq"][:], p_uso[:, 0:128], AF.Square),
                wait=[tO])
        MM("uso2", p_uso[0:1, 128:256], c_ones_col[:], Wk["osq"][:], wait=[so],
           inc=[(TE, 1)])
        tev = (TE, TE.n)
        sr = ES(lambda s: s.activation(b_sr1[:], p_uso[0:1, 128:256], AF.Ln,
                                       bias=c_small[0:1, 12:13],
                                       scale=c_small[0:1, 20:21]), wait=[tev])
        sr2 = ES(lambda s: s.activation(b_sbfr[:], b_sr1[:], AF.Exp,
                                        scale=c_small[0:1, 14:15]), wait=[sr])
        MM("uso3", p_uso[:, 256:384], c_ones_row[:], b_sbfr[:], wait=[sr2],
           inc=[(TE, 1)])
        tev = (TE, TE.n)
        eb = EV(lambda v: v.tensor_copy(Wk["sigbc"][:], p_uso[:, 256:384]), wait=[tev])
        g1 = EV(lambda v: v.scalar_tensor_tensor(
            b_g1[:], p_uso[:, 0:128], c_small[:, 10:11], Wk["sigbc"][:],
            ALU.mult, ALU.mult), wait=[eb])
        gv = EV(lambda v, jj=jj, cs=cs: v.tensor_tensor(
            b_gst[:, jj * TT + cs:jj * TT + cs + C], b_g1[:],
            b_zsil[par][:, jj * TT + cs:jj * TT + cs + C], ALU.mult),
            wait=[g1, cons.get(f"zsil{par}{jj}"), cons.get("gst")])
        consume("uso", gv)
        cons["uso2"] = gv
        cons["uso3"] = gv
        state["prev_head_te"] = (TE, TE.n)

    x_cons = [p0_x_done, None]
    zs_cons = [[None] * 8, [None] * 8]
    ms_cons = [None, None]
    rb_cons = [None, None]
    qkvc_cons = {}
    gst_cons = None
    sQr_c = None
    gq_st = [0, 0, 0, 0]

    def R(n, cs=None, ce=None):
        o = RO[n]
        if cs is None:
            return b_rows[0:8, o:o + TT]
        return b_rows[0:8, o + cs:o + ce]

    for t in range(cfg.NT1):
        par = t % 2
        q = t // cfg.TPQ
        tq = t % cfg.TPQ
        sc.emit("gpsimd", lambda g, t=t, par=par: g.dma_start(
            out=b_x[par][:].rearrange("p (k t) -> p k t", k=KCH),
            in_=xt_v[:, :, t * TT:(t + 1) * TT]),
            wait=[x_cons[par]], inc=[(GX[par], 16)])
        xv = (GX[par], GX[par].n)
        mofs = q * 2 * T4 + tq * TT
        rofs = q * 2 * T4 + T4 + tq * TT
        sc.emit("sync", lambda s, mofs=mofs, par=par: s.dma_start(
            out=b_mrow[par][:], in_=d_stg[0:1, mofs:mofs + TT]),
            wait=[agst, ms_cons[par]], inc=[(MR[par], 16)])
        mrv = (MR[par], MR[par].n)
        sc.emit("sync", lambda s, rofs=rofs: s.dma_start(
            out=b_stbf[0:1, 0:TT], in_=d_stg[0:1, rofs:rofs + TT]),
            wait=[agst, cons.get("stbf")], inc=[(SB1, 16)])
        mv = (SB1, SB1.n)
        MM("uso", p_uso[:, 0:TT], c_ones_row[:], b_stbf[0:1, 0:TT],
           wait=[mv], inc=[(TE, 1)])
        tev = (TE, TE.n)
        rv = EV(lambda v, par=par: v.tensor_copy(b_rstdb[par][:, 0:TT], p_uso[:, 0:TT]),
                wait=[tev, rb_cons[par]])
        consume("uso", rv)
        cons["stbf"] = rv
        rstdv = rv

        bgv = None
        for oc in range(NOC):
            pa = p_acc[oc % 2]
            reg = f"acc{oc % 2}"
            for kc in range(KCH):
                MM(reg, pa[:, 0:TT],
                   b_w[:, kc * NCOL + oc * 128: kc * NCOL + oc * 128 + 128],
                   b_x[par][:, kc * TT:(kc + 1) * TT],
                   start=(kc == 0), stop=False,
                   wait=([xv, rstdv] if kc == 0 else ()))
            MM(reg, pa[:, 0:TT], c_negW1[0:1, oc * 128:oc * 128 + 128],
               b_mrow[par][:], start=False, stop=True, wait=[mrv], inc=[(TE, 1)])
            mmv = (TE, TE.n)
            if oc < 16:
                rb = b_raw[oc % 2]
                tl = b_tail[:, oc * 3:oc * 3 + 3]
                rawc = cons.get(f"raw{oc % 2}")
                g0 = EG(lambda g, rb=rb, tl=tl: g.tensor_copy(rb[:, 0:3], tl),
                        wait=[rawc])
                ev = EV(lambda v, rb=rb, pa=pa, par=par: v.tensor_tensor(
                    rb[:, 3:TT + 3], pa[:, 0:TT], b_rstdb[par][:, 0:TT], ALU.mult),
                    wait=[mmv, g0])
                consume(reg, ev)
                g1 = EG(lambda g, rb=rb, tl=tl: g.tensor_copy(tl, rb[:, TT:TT + 3]),
                        wait=[ev])
                from concourse.bass import broadcast_tensor_aps as _bc

                def cvop(g, out, i0, w_col, add=None):
                    a0, a1 = _bc(i0, w_col)
                    if add is None:
                        return g.tensor_tensor(out, a0, a1, ALU.mult)
                    return None
                g2 = EG(lambda g, rb=rb, oc=oc: g.tensor_tensor(
                    b_zscr[:, 0:TT], *_bc(rb[:, 0:TT], b_cw[:, oc * 4:oc * 4 + 1]),
                    ALU.mult), wait=[ev, cons.get("zscr")])
                g3 = EG(lambda g, rb=rb, oc=oc: g.tensor_tensor(
                    b_zscr2[:, 0:TT], *_bc(rb[:, 1:TT + 1], b_cw[:, oc * 4 + 1:oc * 4 + 2]),
                    ALU.mult), wait=[cons.get("zscr2")])
                g3b = EG(lambda g: g.tensor_tensor(
                    b_zscr[:, 0:TT], b_zscr[:, 0:TT], b_zscr2[:, 0:TT], ALU.add))
                g4 = EG(lambda g, rb=rb, oc=oc: g.tensor_tensor(
                    b_zscr2[:, 0:TT], *_bc(rb[:, 2:TT + 2], b_cw[:, oc * 4 + 2:oc * 4 + 3]),
                    ALU.mult))
                g4b = EG(lambda g: g.tensor_tensor(
                    b_zscr[:, 0:TT], b_zscr[:, 0:TT], b_zscr2[:, 0:TT], ALU.add))
                g5a = EG(lambda g, rb=rb, oc=oc: g.tensor_tensor(
                    b_zscr2[:, 0:TT], *_bc(rb[:, 3:TT + 3], b_cw[:, oc * 4 + 3:oc * 4 + 4]),
                    ALU.mult))
                g5 = EG(lambda g, oc=oc: g.tensor_tensor(
                    b_qkvc[:, oc * TT:(oc + 1) * TT], b_zscr[:, 0:TT],
                    b_zscr2[:, 0:TT], ALU.add),
                    wait=[qkvc_cons.get(oc)])
                cons[f"raw{oc % 2}"] = g5
                cons["zscr"] = g5
                cons["zscr2"] = g5
                cons[f"qkvc{oc}"] = g5
            elif oc < 24:
                zi = oc - 16
                ev = EV(lambda v, pa=pa, par=par: v.tensor_tensor(
                    b_zscr[:, 0:TT], pa[:, 0:TT], b_rstdb[par][:, 0:TT], ALU.mult),
                    wait=[mmv, cons.get("zscr"), zs_cons[par][zi]])
                consume(reg, ev)
                sv = ES(lambda s, zi=zi, par=par: s.activation(
                    b_zsil[par][:, zi * TT:(zi + 1) * TT], b_zscr[:, 0:TT], AF.Silu),
                    wait=[ev])
                cons["zscr"] = sv
                cons[f"zsil{par}{zi}"] = sv
            else:
                v1 = EV(lambda v, pa=pa, par=par: v.tensor_tensor(
                    R("lnb"), pa[0:8, 0:TT], b_rstdb[par][0:8, 0:TT], ALU.mult),
                    wait=[mmv, cons.get("rows")])
                v2 = EV(lambda v, pa=pa, par=par: v.tensor_tensor(
                    R("lg"), pa[32:40, 0:TT], b_rstdb[par][0:8, 0:TT], ALU.mult))
                consume(reg, v2)
                bgv = v2

        s1 = ES(lambda s: s.activation(R("t0"), R("lnb"), AF.Abs), wait=[bgv])
        s2 = ES(lambda s: s.activation(R("t0"), R("t0"), AF.Exp, scale=c_small[0:8, 21:22]))
        s3 = ES(lambda s: s.activation(R("t0"), R("t0"), AF.Ln, bias=c_small[0:8, 19:20]))
        s4 = ES(lambda s: s.activation(R("lnb"), R("lnb"), AF.Relu, scale=c_small[0:8, 21:22]))
        v1 = EV(lambda v: v.scalar_tensor_tensor(R("lnb"), R("lnb"), -1.0, R("t0"),
                                                 ALU.mult, ALU.subtract), wait=[s4])
        s5 = ES(lambda s: s.activation(R("t0"), R("lg"), AF.Abs, bias=c_small[0:8, 8:9]),
                wait=[v1])
        s6 = ES(lambda s: s.activation(R("t0"), R("t0"), AF.Exp, scale=c_small[0:8, 21:22]))
        s7 = ES(lambda s: s.activation(R("t0"), R("t0"), AF.Ln, bias=c_small[0:8, 19:20]))
        s8 = ES(lambda s: s.activation(R("lg"), R("lg"), AF.Relu, bias=c_small[0:8, 8:9]))
        v2 = EV(lambda v: v.tensor_tensor(R("lg"), R("lg"), R("t0"), ALU.add), wait=[s8])
        v3 = EV(lambda v: v.tensor_scalar(R("lg"), R("lg"), c_expAn[0:8, 0:1], None, ALU.mult))
        v4 = EV(lambda v: v.tensor_tensor_scan(R("L"), R("lg"), R("lg"),
                                               b_carry[0:8, 0:1], ALU.add, ALU.bypass),
                wait=[v3, zinit])
        vv = v4
        for cc_ in range(CPT):
            cs, ce = cc_ * C, (cc_ + 1) * C
            if cc_ == 0:
                vv = EV(lambda v, cs=cs, ce=ce: v.tensor_scalar(
                    R("Lrel", cs, ce), R("L", cs, ce), b_carry[0:8, 0:1], None,
                    ALU.subtract), wait=[vv])
            else:
                vv = EV(lambda v, cs=cs, ce=ce: v.tensor_scalar(
                    R("Lrel", cs, ce), R("L", cs, ce),
                    b_rows[0:8, RO["L"] + cs - 1:RO["L"] + cs], None, ALU.subtract))
        v5 = EV(lambda v: v.tensor_copy(b_carry[0:8, 0:1],
                                        b_rows[0:8, RO["L"] + TT - 1:RO["L"] + TT]),
                wait=[vv])
        v6 = EV(lambda v: v.tensor_tensor(R("lg"), R("Lrel"), R("lg"), ALU.subtract))
        for cc_ in range(CPT):
            cs, ce = cc_ * C, (cc_ + 1) * C
            EV(lambda v, cs=cs, ce=ce: v.tensor_scalar(
                R("lendf", cs, ce), R("Lrel", cs, ce), 0.0,
                b_rows[0:8, RO["Lrel"] + ce - 1:RO["Lrel"] + ce], ALU.mult, ALU.add))
        for h in range(8):
            oc = (4 + h) if h < 4 else (h - 4)
            sev = ES(lambda s, oc=oc: s.activation(
                b_sq[:, 0:TT], b_qkvc[:, oc * TT:(oc + 1) * TT], AF.Square),
                wait=[cons.get(f"qkvc{oc}"), (TE, TE.n) if h else None])
            MM("kq", p_kq[0:1, 0:TT], c_ones_col[:], b_sq[:, 0:TT], wait=[sev],
               inc=[(TE, 1)])
            tev = (TE, TE.n)
            s9 = ES(lambda s, h=h: s.activation(
                b_lnrp[0:1, h * TT:(h + 1) * TT], p_kq[0:1, 0:TT], AF.Ln,
                bias=c_small[0:1, 13:14]), wait=[tev])
            consume("kq", s9)
        lnv = (SE, SE.n)
        for dst, off in ((b_lnrkJ, 0), (b_lnrqJ, 4)):
            for h in range(4):
                MM("uso", p_uso[0:8, 0:TT], c_sel8[0:1, h * 8:h * 8 + 8],
                   b_lnrp[0:1, (off + h) * TT:(off + h + 1) * TT],
                   start=(h == 0), stop=(h == 3),
                   wait=([lnv] if h == 0 else ()), inc=([(TE, 1)] if h == 3 else ()))
            tev = (TE, TE.n)
            v = EV(lambda v, dst=dst: v.tensor_copy(dst[0:8, 0:TT], p_uso[0:8, 0:TT]),
                   wait=[tev])
            consume("uso", v)
        va = EV(lambda v: v.tensor_tensor(R("colA"), b_lnrkJ[0:8, 0:TT], R("Lrel"),
                                          ALU.subtract))
        vb = EV(lambda v: v.tensor_tensor(R("rowA"), R("lnb"), R("lg"), ALU.add))
        vc = EV(lambda v: v.tensor_tensor(R("rowA"), R("rowA"), b_lnrkJ[0:8, 0:TT], ALU.add))
        vd = EV(lambda v: v.tensor_tensor(R("rowO"), R("Lrel"), b_lnrqJ[0:8, 0:TT], ALU.add))
        ve = EV(lambda v: v.tensor_tensor(R("sk3a"), R("colA"), R("lendf"), ALU.add))
        sqv = ES(lambda s: s.activation(b_sqr[0:8, 0:TT], R("rowO"), AF.Exp),
                 wait=[vd, sQr_c])
        state["sqv"] = sqv
        rows_done = EV(lambda v: v.tensor_copy(b_strow[0:1, 0:1], b_rows[0:1, 0:1]),
                       wait=[va, vb, vc, vd, ve, v5, v6])
        state["rows_done"] = rows_done

        for cc_ in range(CPT):
            cs = cc_ * C
            for h in range(4):
                kofs = (4 + h) * TT + cs
                qofs = h * TT + cs
                ksl = b_qkvc[:, kofs:kofs + C]
                qsl = b_qkvc[:, qofs:qofs + C]
                TRP("bf", p_bf[:, 0:128], ksl,
                    wait=[cons.get(f"qkvc{4 + h}"), state["rows_done"]],
                    inc=[(TE, 1)])
                tev = (TE, TE.n)
                e1 = EV(lambda v: v.tensor_copy(Wk["ktm"][:], p_bf[:, 0:128]),
                        wait=[tev])
                consume("bf", e1)
                MM("kq", p_kq[:, 0:128], ksl, ksl, inc=[(TE, 1)])
                MM("kq", p_kq[:, 128:256], ksl, qsl,
                   wait=[cons.get(f"qkvc{h}")], inc=[(TE, 1)])
                tev = (TE, TE.n)
                e2 = EV(lambda v: v.tensor_copy(Wk["kk"][:], p_kq[:, 0:128]),
                        wait=[tev, cons.get("kk")])
                e3 = EV(lambda v: v.tensor_copy(Wk["kqt"][:], p_kq[:, 128:256]),
                        wait=[cons.get("kqt")])
                consume("kq", e3)
                e4 = EV(lambda v: v.tensor_tensor(Wk["kkbd"][:], Wk["kk"][:],
                                                  c_bd64[:], ALU.mult),
                        wait=[cons.get("kkbd")])
                e5 = EV(lambda v: v.tensor_tensor(Wk["kkoff"][:], Wk["kk"][:],
                                                  c_offn[:], ALU.mult),
                        wait=[cons.get("kkoff")])
                h_done = (e1, e2, e3, e4, e5)

                for jj in (2 * h, 2 * h + 1):
                    scan_head(t, par, cc_, cs, h, jj, ksl, qsl, h_done)

        scv = (TE, TE.n)
        sev = (SE, SE.n)
        dvv = (DV, DV.n)
        for oc in range(16):
            qkvc_cons[oc] = scv
        for zi in range(8):
            zs_cons[par][zi] = dvv
        x_cons[par] = scv
        ms_cons[par] = scv
        rb_cons[par] = dvv
        cons["rows"] = scv
        sQr_c = scv
        SEG = min(TT, T16)
        for jj in range(8):
            w = [dvv]
            for so in range(0, TT, SEG):
                sub = (tq * TT + so) // T16
                c0 = (tq * TT + so) % T16
                sc.emit("sync", lambda s, jj=jj, q=q, sub=sub, c0=c0, so=so: s.dma_start(
                    out=d_gq[q].rearrange("o (s r t) -> (o s) r t", s=4, t=T16)
                        [sub, jj * 128:(jj + 1) * 128, c0:c0 + SEG],
                    in_=b_gst[:, jj * TT + so:jj * TT + so + SEG]),
                    wait=w, inc=[(ST, 16)])
        gst_cons = (ST, ST.n)
        cons["gst"] = gst_cons
        if cfg.debug:
            for oc in range(16):
                sc.emit("sync", lambda s, oc=oc, t=t: s.dma_start(
                    out=dbg["qkvc"].rearrange("p (o t) -> p o t", o=16)[:, oc, t * TT:(t + 1) * TT],
                    in_=b_qkvc[:, oc * TT:(oc + 1) * TT]), wait=[scv], inc=[(ST, 16)])
            for jj in range(8):
                sc.emit("sync", lambda s, jj=jj, t=t: s.dma_start(
                    out=dbg["gout"].rearrange("p (o t) -> p o t", o=8)[:, jj, t * TT:(t + 1) * TT],
                    in_=b_gst[:, jj * TT:(jj + 1) * TT]), wait=[dvv], inc=[(ST, 16)])
            gst_cons = (ST, ST.n)
            cons["gst"] = gst_cons
        if tq == cfg.TPQ - 1:
            gq_st[q] = ST.n
            sc.emit("gpsimd", lambda g, q=q: g.collective_compute(
                "AllGather", ALU.bypass, replica_groups=[[0, 1, 2, 3], [4, 5, 6, 7]],
                ins=[d_gq[q]], outs=[d_gg[q]]),
                wait=[(ST, gq_st[q])], inc=[(CCs, 1)])
            agv = (CCs, CCs.n)
            CH = 1024 * T16
            for m in range(4):
                sc.emit("sync", lambda s, q=q, m=m, CH=CH: s.dma_start(
                    out=d_gsl[q][0:1, m * CH:(m + 1) * CH],
                    in_=d_gg[q][0:1, :][:, bass.ts(rcell[m], CH)]),
                    wait=[agv], inc=[(RS[q], 16)])
            gq_st[q] = RS[q].n

    ag_q = [(RS[q], gq_st[q]) for q in range(4)]

    for g in reversed(p1g):
        g.__exit__(None, None, None)
    snap = [(TE, TE.n), (DV, DV.n), (SE, SE.n), (GC, GC.n),
            (SD, SD.n), (GD, GD.n), (ST, ST.n), (S2, S2.n)]
    sc.emit("tensor", lambda t: t.nop(), wait=[snap[1], snap[2], snap[3]], inc=[(TE, 1)])
    sc.emit("vector", lambda v: v.nop(), wait=[snap[0], snap[2], snap[3]], inc=[(DV, 1)])
    sc.emit("scalar", lambda s: s.nop(), wait=[snap[0], snap[1], snap[3]], inc=[(SE, 1)])
    sc.emit("gpsimd", lambda g: g.nop(), wait=[snap[0], snap[1], snap[2]], inc=[(GC, 1)])
    sc.emit("sync", lambda s: s.nop(), wait=[snap[0], snap[1], snap[2], snap[3]])
    barrier = [(TE, TE.n), (DV, DV.n), (SE, SE.n), (GC, GC.n)]

    p2g = []
    def A2(n, s, dt):
        g = nc.sbuf_tensor(n, s, dt)
        h = g.__enter__()
        p2g.append(g)
        return h

    b_x1 = A2("p2_x1", [128, 16 * P2TT], F32)
    b_g2 = A2("p2_g2", [128, 32 * P2TT], BF16)
    b_midhi = A2("p2_midhi", [128, 32 * P2TT], BF16)
    def mids_ap(c):
        base = b_g2 if c < 32 else b_midhi
        return base[:, (c % 32) * P2TT:((c % 32) + 1) * P2TT]
    b_yf = A2("p2_yf", [128, 16 * P2TT], BF16)
    b_woall = A2("p2_woall", [128, 64 * 128], BF16)
    b_wo = [b_woall[:, 0:32 * 128], b_woall[:, 32 * 128:64 * 128]]
    b_w13 = [A2(f"p2_w13{i}", [128, 16 * 128], BF16) for i in range(4)]
    b_w2s = [A2(f"p2_w2s{i}", [128, 64 * 128], BF16) for i in range(2)]
    b_x2 = [A2(f"p2_x2{i}", [128, P2TT], F32) for i in range(2)]
    b_hid = A2("p2_hid", [128, P2TT], BF16)
    b_hid2 = A2("p2_hid2", [128, P2TT], BF16)
    b_bc1 = A2("p2_bc1", [128, P2TT], F32)
    b_bc2 = A2("p2_bc2", [128, P2TT], F32)
    b_r1 = A2("p2_r1", [1, P2TT], F32)
    b_r2 = A2("p2_r2", [1, P2TT], F32)
    b_r3 = A2("p2_r3", [1, P2TT], F32)
    b_sq2b = A2("p2_sq2b", [128, P2TT], BF16)
    b_x2o = A2("p2_x2o", [128, P2TT], F32)
    b_ln2 = b_woall

    gsl_v = [d_gsl[q].rearrange("o (r t) -> (o r) t", t=T16) for q in range(4)]
    xp2_v = d_xp2.rearrange("p (k t) -> p k t", k=KCH)
    out_v = d_out.rearrange("p (k t) -> p k t", k=KCH)
    wobf_v = d_wobf
    w13_v = d_w13bf
    w2t_rv = d_w2t

    def stats_rows(psum_s1, psum_s2, t1mark, eps_col):
        v1 = EV(lambda v: v.tensor_scalar(b_r1[:], psum_s1, c_small[0:1, 16:17],
                                          None, ALU.mult), wait=[t1mark])
        v2 = EV(lambda v: v.scalar_tensor_tensor(b_r3[:], b_r1[:], -1.0, b_r1[:],
                                                 ALU.mult, ALU.mult), wait=[v1])
        v3 = EV(lambda v: v.scalar_tensor_tensor(b_r3[:], psum_s2, 1.0 / 2048.0,
                                                 b_r3[:], ALU.mult, ALU.add), wait=[v2])
        s1 = ES(lambda s: s.activation(b_r3[:], b_r3[:], AF.Ln, bias=eps_col), wait=[v3])
        s2 = ES(lambda s: s.activation(b_r2[:], b_r3[:], AF.Exp,
                                       scale=c_small[0:1, 14:15]), wait=[s1])
        MM("uso", p_uso[:, 0:P2TT], c_ones_rowF[0:1, 0:128], b_r1[:], wait=[v1],
           inc=[(TE, 1)])
        t_a = (TE, TE.n)
        e1 = EV(lambda v: v.tensor_copy(b_bc1[:], p_uso[:, 0:P2TT]), wait=[t_a])
        consume("uso", e1)
        MM("uso", p_uso[:, 0:P2TT], c_ones_rowF[0:1, 0:128], b_r2[:], wait=[s2],
           inc=[(TE, 1)])
        t_b = (TE, TE.n)
        e2 = EV(lambda v: v.tensor_copy(b_bc2[:], p_uso[:, 0:P2TT]), wait=[t_b])
        consume("uso", e2)
        return (v3, e1, e2)

    for i2 in range(cfg.NT2):
        q = (i2 * P2TT) // T16
        agq = ag_q[q]
        c0 = (i2 * P2TT) % T16
        for kc in range(32):
            sc.emit("sync", lambda s, kc=kc, q=q, c0=c0: s.dma_start(
                out=b_g2[:, kc * P2TT:(kc + 1) * P2TT],
                in_=gsl_v[q][kc * 128:(kc + 1) * 128, c0:c0 + P2TT]),
                wait=[agq, barrier[0], cons.get("g2")], inc=[(G2S, 16)])
        glv = (G2S, G2S.n)
        for oc in range(16):
            wb = b_wo[oc % 2]
            sc.emit("sync", lambda s, oc=oc, wb=wb: s.dma_start(
                out=wb[:], in_=wobf_v[:, oc * 4096:(oc + 1) * 4096]),
                wait=[cons.get(f"wo{oc % 2}")], inc=[(WOS[oc % 2], 16)])
            wv = (WOS[oc % 2], WOS[oc % 2].n)
            pa = p_acc[oc % 2]
            reg = f"acc{oc % 2}"
            for kc in range(32):
                MM(reg, pa[:, 0:P2TT], wb[:, kc * 128:(kc + 1) * 128],
                   b_g2[:, kc * P2TT:(kc + 1) * P2TT],
                   start=(kc == 0), stop=(kc == 31),
                   wait=([glv, wv, barrier[0]] if kc == 0 else ()),
                   inc=([(TE, 1)] if kc == 31 else ()))
            mmv = (TE, TE.n)
            cons[f"wo{oc % 2}"] = mmv
            ev = EV(lambda v, pa=pa, oc=oc: v.tensor_copy(
                b_yf[:, oc * P2TT:(oc + 1) * P2TT], pa[:, 0:P2TT]),
                wait=[mmv, barrier[1], cons.get(f"yf{oc}")])
            consume(reg, ev)
            MM("kq", p_kq[0:1, 0:P2TT], c_ones_col[:],
               b_yf[:, oc * P2TT:(oc + 1) * P2TT], start=(oc == 0), stop=(oc == 15),
               wait=[ev], inc=([(TE, 1)] if oc == 15 else ()))
            sv = ES(lambda s, oc=oc: s.activation(
                b_sq2b[:], b_yf[:, oc * P2TT:(oc + 1) * P2TT], AF.Square),
                wait=[ev, barrier[2], cons.get("sq2b")])
            MM("ap", p_ap[0:1, 0:P2TT], c_ones_col[:], b_sq2b[:],
               start=(oc == 0), stop=(oc == 15), wait=[sv],
               inc=([(TE, 1)] if oc == 15 else ()))
            cons["sq2b"] = (TE, TE.n)
        t1m = (TE, TE.n)
        sm = stats_rows(p_kq[0:1, 0:P2TT], p_ap[0:1, 0:P2TT], t1m,
                        c_small[0:1, 15:16])
        consume("kq", sm[0])
        consume("ap", sm[0])
        x1dv = []
        for oc in range(16):
            xs = b_x2[oc % 2]
            sc.emit("sync", lambda s, oc=oc, xs=xs, i2=i2: s.dma_start(
                out=xs[:], in_=xp2_v[:, oc, i2 * P2TT:(i2 + 1) * P2TT]),
                wait=[cons.get(f"x2{oc % 2}")], inc=[(X2S[oc % 2], 16)])
            xv = (X2S[oc % 2], X2S[oc % 2].n)
            v1 = EV(lambda v, oc=oc: v.tensor_tensor(
                b_x1[:, oc * P2TT:(oc + 1) * P2TT],
                b_yf[:, oc * P2TT:(oc + 1) * P2TT], b_bc1[:], ALU.subtract),
                wait=[sm[1], sm[2]])
            v2 = EV(lambda v, oc=oc: v.tensor_tensor(
                b_x1[:, oc * P2TT:(oc + 1) * P2TT],
                b_x1[:, oc * P2TT:(oc + 1) * P2TT], b_bc2[:], ALU.mult))
            v3 = EV(lambda v, oc=oc, xs=xs: v.tensor_tensor(
                b_x1[:, oc * P2TT:(oc + 1) * P2TT],
                b_x1[:, oc * P2TT:(oc + 1) * P2TT], xs[:], ALU.add), wait=[xv])
            cons[f"x2{oc % 2}"] = v3
            cons[f"yf{oc}"] = v1
            x1dv.append(v3)
        for oc in range(16):
            cv = EV(lambda v, oc=oc: v.tensor_copy(
                b_ln2[:, oc * P2TT:(oc + 1) * P2TT],
                b_x1[:, oc * P2TT:(oc + 1) * P2TT]), wait=[x1dv[oc], cons.get(f"ln2{oc}")])
            MM("kq", p_kq[0:1, 0:P2TT], c_ones_col[:],
               b_ln2[:, oc * P2TT:(oc + 1) * P2TT], start=(oc == 0), stop=(oc == 15),
               wait=[cv], inc=([(TE, 1)] if oc == 15 else ()))
            sv = ES(lambda s, oc=oc: s.activation(
                b_sq2b[:], b_ln2[:, oc * P2TT:(oc + 1) * P2TT], AF.Square),
                wait=[cv, cons.get("sq2b")])
            MM("ap", p_ap[0:1, 0:P2TT], c_ones_col[:], b_sq2b[:],
               start=(oc == 0), stop=(oc == 15), wait=[sv],
               inc=([(TE, 1)] if oc == 15 else ()))
            cons["sq2b"] = (TE, TE.n)
        t2m = (TE, TE.n)
        sm2 = stats_rows(p_kq[0:1, 0:P2TT], p_ap[0:1, 0:P2TT], t2m,
                         c_small[0:1, 15:16])
        consume("kq", sm2[0])
        consume("ap", sm2[0])
        for oc in range(16):
            v1 = EV(lambda v, oc=oc: v.tensor_tensor(
                b_x2o[:], b_x1[:, oc * P2TT:(oc + 1) * P2TT], b_bc1[:], ALU.subtract),
                wait=[sm2[1], sm2[2], cons.get("x2o")])
            v2 = EV(lambda v, oc=oc: v.tensor_tensor(
                b_ln2[:, oc * P2TT:(oc + 1) * P2TT], b_x2o[:], b_bc2[:], ALU.mult))
            cons["x2o"] = v2
            cons[f"ln2{oc}"] = v2
        ln2v = (DV, DV.n)
        for hp in range(64):
            wgt = b_w13[(2 * hp) % 4]
            wlt = b_w13[(2 * hp + 1) % 4]
            sc.emit("sync", lambda s, hp=hp, wgt=wgt: s.dma_start(
                out=wgt[:], in_=w13_v[:, hp * 2048:(hp + 1) * 2048]),
                wait=[cons.get(f"w13{(2 * hp) % 4}")], inc=[(W13S[(2 * hp) % 4], 16)])
            wva = (W13S[(2 * hp) % 4], W13S[(2 * hp) % 4].n)
            sc.emit("sync", lambda s, hp=hp, wlt=wlt: s.dma_start(
                out=wlt[:], in_=w13_v[:, (64 + hp) * 2048:(64 + hp + 1) * 2048]),
                wait=[cons.get(f"w13{(2 * hp + 1) % 4}")], inc=[(W13S[(2 * hp + 1) % 4], 16)])
            wvb = (W13S[(2 * hp + 1) % 4], W13S[(2 * hp + 1) % 4].n)
            for kc in range(KCH):
                MM("acc0", p_acc[0][:, 0:P2TT], wgt[:, kc * 128:(kc + 1) * 128],
                   b_ln2[:, kc * P2TT:(kc + 1) * P2TT],
                   start=(kc == 0), stop=(kc == 15),
                   wait=([wva, ln2v] if kc == 0 else ()),
                   inc=([(TE, 1)] if kc == 15 else ()))
            tg = (TE, TE.n)
            cons[f"w13{(2 * hp) % 4}"] = tg
            for kc in range(KCH):
                MM("acc1", p_acc[1][:, 0:P2TT], wlt[:, kc * 128:(kc + 1) * 128],
                   b_ln2[:, kc * P2TT:(kc + 1) * P2TT],
                   start=(kc == 0), stop=(kc == 15),
                   wait=([wvb] if kc == 0 else ()),
                   inc=([(TE, 1)] if kc == 15 else ()))
            tl = (TE, TE.n)
            cons[f"w13{(2 * hp + 1) % 4}"] = tl
            sv = ES(lambda s: s.activation(b_hid[:], p_acc[0][:, 0:P2TT], AF.Silu),
                    wait=[tg, cons.get("hid")])
            consume("acc0", sv)
            hv = EV(lambda v: v.tensor_tensor(b_hid2[:], b_hid[:],
                                              p_acc[1][:, 0:P2TT], ALU.mult),
                    wait=[sv, tl, cons.get("hid2")])
            consume("acc1", hv)
            cons["hid"] = hv
            MM("sq", p_sq[:, 0:P2TT], c_h128[:], b_hid2[:], wait=[hv], inc=[(TE, 1)])
            th = (TE, TE.n)
            cons["hid2"] = th
            mv = EV(lambda v, hp=hp: v.tensor_copy(
                mids_ap(hp), p_sq[:, 0:P2TT]),
                wait=[th, cons.get(f"mid{hp}")])
            consume("sq", mv)
        cons["wo0"] = tl
        cons["wo1"] = tl
        for oc in range(16):
            ws = b_w2s[oc % 2]
            sc.emit("sync", lambda s, oc=oc, ws=ws: s.dma_start(
                out=ws[:], in_=w2t_rv[:, oc * 8192:(oc + 1) * 8192]),
                wait=[cons.get(f"w2s{oc % 2}")], inc=[(W2TS[oc % 2], 16)])
            wv = (W2TS[oc % 2], W2TS[oc % 2].n)
            pa = p_acc[oc % 2]
            reg = f"acc{oc % 2}"
            for kc in range(64):
                MM(reg, pa[:, 0:P2TT], ws[:, kc * 128:(kc + 1) * 128],
                   mids_ap(kc),
                   start=(kc == 0), stop=(kc == 63),
                   wait=([wv, (DV, DV.n)] if kc == 0 else ()),
                   inc=([(TE, 1)] if kc == 63 else ()))
            mmv = (TE, TE.n)
            cons[f"w2s{oc % 2}"] = mmv
            ev = EV(lambda v, pa=pa, oc=oc: v.tensor_copy(
                b_yf[:, oc * P2TT:(oc + 1) * P2TT], pa[:, 0:P2TT]),
                wait=[mmv, cons.get(f"yf{oc}")])
            consume(reg, ev)
            MM("kq", p_kq[0:1, 0:P2TT], c_ones_col[:],
               b_yf[:, oc * P2TT:(oc + 1) * P2TT], start=(oc == 0), stop=(oc == 15),
               wait=[ev], inc=([(TE, 1)] if oc == 15 else ()))
            sv = ES(lambda s, oc=oc: s.activation(
                b_sq2b[:], b_yf[:, oc * P2TT:(oc + 1) * P2TT], AF.Square),
                wait=[ev, cons.get("sq2b")])
            MM("ap", p_ap[0:1, 0:P2TT], c_ones_col[:], b_sq2b[:],
               start=(oc == 0), stop=(oc == 15), wait=[sv],
               inc=([(TE, 1)] if oc == 15 else ()))
            cons["sq2b"] = (TE, TE.n)
        t3m = (TE, TE.n)
        sm3 = stats_rows(p_kq[0:1, 0:P2TT], p_ap[0:1, 0:P2TT], t3m,
                         c_small[0:1, 15:16])
        consume("kq", sm3[0])
        consume("ap", sm3[0])
        for hp in range(64):
            cons[f"mid{hp}"] = t3m
        for oc in range(16):
            v1 = EV(lambda v, oc=oc: v.tensor_tensor(
                b_x2o[:], b_yf[:, oc * P2TT:(oc + 1) * P2TT], b_bc1[:], ALU.subtract),
                wait=[sm3[1], sm3[2], cons.get("x2o")])
            v2 = EV(lambda v: v.tensor_tensor(b_x2o[:], b_x2o[:], b_bc2[:], ALU.mult))
            v3 = EV(lambda v, oc=oc: v.tensor_tensor(
                b_x2o[:], b_x2o[:], b_x1[:, oc * P2TT:(oc + 1) * P2TT], ALU.add))
            cons["x2o"] = v3
            cons[f"yf{oc}"] = v3
            cons[f"ln2{oc}"] = v3
            sc.emit("sync", lambda s, oc=oc, i2=i2: s.dma_start(
                out=out_v[:, oc, i2 * P2TT:(i2 + 1) * P2TT], in_=b_x2o[:]),
                wait=[v3], inc=[(ST, 16)])
            cons["x2o"] = (ST, ST.n)
        cons["g2"] = (TE, TE.n)

    sc.emit("sync", lambda s: s.nop(), wait=[(ST, ST.n)])

    return nc, sc, dict(dbg=dbg)


_CACHE = {}


def kernel(**inputs):
    cfg = Cfg(T=16384, debug=False)
    key = (cfg.T, cfg.debug)
    if key not in _CACHE:
        nc, sc, aux = build(cfg)
        for s in sc.sems.values():
            s.h = nc.alloc_semaphore(s.name)
        with nc.Block() as block:
            sc.replay(block)
        _CACHE[key] = (nc, aux)
    nc, aux = _CACHE[key]
    in_maps = shard_inputs(cfg, inputs)
    res = run_bass_kernel_spmd(nc, in_maps, core_ids=list(range(8)))
    T, T4, T16 = cfg.T, cfg.T4, cfg.T16
    out = np.empty((2, T, 2048), np.float32)
    for c in range(8):
        b, g = c // 4, c % 4
        y = res.results[c]["yout"].reshape(128, 16, 4 * T16)
        y = y.transpose(1, 0, 2).reshape(2048, 4 * T16)
        for q in range(4):
            sl = slice(q * T4 + g * T16, q * T4 + (g + 1) * T16)
            out[b, sl, :] = y[:, q * T16:(q + 1) * T16].T
    return out


# version 03 (final)
# speedup vs baseline: 1.0214x; 1.0214x over previous
import math
import numpy as np
import concourse.bass as bass
import concourse.mybir as mybir
from concourse.bass_utils import run_bass_kernel_spmd

F32 = mybir.dt.float32
BF16 = mybir.dt.bfloat16
U32 = mybir.dt.uint32
AF = mybir.ActivationFunctionType
ALU = mybir.AluOpType

NEG = -30000.0


class Cfg:
    def __init__(self, T=16384, debug=False):
        assert T % 2048 == 0
        self.T = T
        self.TT = 256
        self.C = 128
        self.CPT = self.TT // self.C
        self.T4 = T // 4
        self.T16 = T // 16
        self.NT1 = T // self.TT
        self.TPQ = self.NT1 // 4
        self.P2TT = min(512, self.T16)
        self.NT2 = (4 * self.T16) // self.P2TT
        self.KCH = 16
        self.NCOL = 3200
        self.NOC = 25
        self.debug = debug


class Sem:
    __slots__ = ("name", "n", "h")
    def __init__(self, name):
        self.name, self.n, self.h = name, 0, None


class Sched:
    ENGINES = ("sync", "gpsimd", "tensor", "vector", "scalar")

    def __init__(self):
        self.ops = []
        self.sems = {}
        self.marks = []

    def mark(self, label):
        n = sum(1 for (e, _, _, _) in self.ops if e == "tensor")
        self.marks.append((n, label))

    def sem(self, name):
        s = self.sems.get(name)
        if s is None:
            s = self.sems[name] = Sem(name)
        return s

    def emit(self, eng, fn, wait=(), inc=()):
        ii = []
        for (s, d) in inc:
            s.n += d
            ii.append((s, d))
        self.ops.append((eng, fn, [w for w in wait if w is not None], ii))

    def replay(self, block):
        last = {e: {} for e in self.ENGINES}

        def mk(name):
            def run(eng):
                lw = last[name]
                for (e, fn, waits, incs) in self.ops:
                    if e != name:
                        continue
                    for (s, v) in waits:
                        if lw.get(s.name, -1) >= v:
                            continue
                        eng.wait_ge(s.h, v)
                        lw[s.name] = v
                    inst = fn(eng)
                    for (s, d) in incs:
                        inst.then_inc(s.h, d)
            return run

        block.sync(mk("sync"))
        block.gpsimd(mk("gpsimd"))
        block.tensor(mk("tensor"))
        block.vector(mk("vector"))
        block.scalar(mk("scalar"))


def _hadamard(n):
    H = np.array([[1.0]], np.float32)
    while H.shape[0] < n:
        H = np.block([[H, H], [H, -H]])
    return H


def shard_inputs(cfg, inp):
    T = cfg.T
    x = np.asarray(inp["L0_attn_x"], np.float32)[:, :T, :]
    Wqkv = np.asarray(inp["L0_attn_W_qkv"], np.float32)
    Wz = np.asarray(inp["L0_attn_W_z"], np.float32)
    Wb = np.asarray(inp["L0_attn_W_b"], np.float32)
    Wa = np.asarray(inp["L0_attn_W_a"], np.float32)
    convw = np.asarray(inp["L0_attn_conv_w"], np.float32)
    dtb = np.asarray(inp["L0_attn_dt_bias"], np.float32)
    alog = np.asarray(inp["L0_attn_A_log"], np.float32)
    normw = np.asarray(inp["L0_attn_norm_w"], np.float32)
    Wout = np.asarray(inp["L0_attn_W_out"], np.float32)
    W1 = np.asarray(inp["L0_ffn_W1"], np.float32)
    W2 = np.asarray(inp["L0_ffn_W2"], np.float32)
    W3 = np.asarray(inp["L0_ffn_W3"], np.float32)

    i1 = np.arange(128)
    bd64s = (i1[:, None] // 64 == i1[None, :] // 64).astype(np.float32)
    ejmat = np.zeros((128, 1024), np.float32)
    for j in range(8):
        ejmat[j, j * 128:(j + 1) * 128] = 1.0
    cf = np.concatenate([
        np.eye(128, dtype=np.float32),
        bd64s,
        -(1.0 - bd64s),
        np.where(i1[:, None] > i1[None, :], 0.0, NEG),
        np.where(i1[:, None] < i1[None, :], 0.0, NEG),
        np.where(i1[:, None] <= i1[None, :], 0.0, NEG),
        _hadamard(128) * (2.0 ** -7),
        ejmat,
    ], axis=1).astype(np.float32)

    smalls = np.zeros((128, 24), np.float32)
    smalls[0:8, 0:8] = np.eye(8)
    smalls[:, 11] = math.sqrt(2.0)
    smalls[:, 12] = 1e-6
    smalls[:, 13] = 1e-24
    smalls[:, 14] = -0.5
    smalls[:, 15] = 1e-5
    smalls[:, 16] = 1.0 / 2048.0
    smalls[:, 17] = 2.0
    smalls[:, 18] = -math.log(128.0)
    smalls[:, 19] = 1.0
    smalls[:, 20] = 1.0 / 128.0
    smalls[:, 21] = -1.0

    h64 = _hadamard(64)
    sel8 = np.zeros((1, 32), np.float32)
    for h in range(4):
        sel8[0, h * 8 + 2 * h] = -0.5
        sel8[0, h * 8 + 2 * h + 1] = -0.5

    in_maps = []
    for c in range(8):
        b, g = c // 4, c % 4
        js = np.arange(8 * g, 8 * g + 8)
        hs = np.arange(4 * g, 4 * g + 4)
        qcols = (hs[:, None] * 128 + np.arange(128)).reshape(-1)
        vcols = (js[:, None] * 128 + np.arange(128)).reshape(-1)
        bg = np.zeros((128, 2048), np.float32)
        bg[0:8] = Wb[js]
        bg[32:40] = Wa[js]
        Wbig = np.concatenate([Wqkv[qcols], Wqkv[2048 + qcols],
                               Wqkv[4096 + vcols], Wz[vcols], bg], axis=0)
        wbig = np.ascontiguousarray(
            Wbig.T.reshape(16, 128, cfg.NCOL).transpose(1, 0, 2)
                 .reshape(128, 16 * cfg.NCOL), np.float32)

        cw = np.zeros((128, 64), np.float32)
        qkvcols = np.concatenate([qcols, 2048 + qcols, 4096 + vcols])
        for kc in range(16):
            cw[:, kc * 4:(kc + 1) * 4] = convw[qkvcols[kc * 128:(kc + 1) * 128]]

        xt = np.ascontiguousarray(
            x[b].T.reshape(16, 128, T).transpose(1, 0, 2).reshape(128, 16 * T),
            np.float32)
        xq = np.ascontiguousarray(
            xt.reshape(128, 16, T)[:, :, g * cfg.T4:(g + 1) * cfg.T4]
              .reshape(128, 16 * cfg.T4))
        p2tok = np.concatenate([np.arange(q * cfg.T4 + g * cfg.T16,
                                          q * cfg.T4 + (g + 1) * cfg.T16)
                                for q in range(4)])
        xp2 = np.ascontiguousarray(
            xt.reshape(128, 16, T)[:, :, p2tok].reshape(128, 16 * 4 * cfg.T16))

        woutT = Wout.T.reshape(32, 128, 16, 128).transpose(1, 2, 0, 3) \
                      .reshape(128, 16 * 32 * 128)
        w13 = np.concatenate([W1, W3], axis=0)
        w13T = w13.T.reshape(16, 128, 128, 128).transpose(1, 2, 0, 3) \
                    .reshape(128, 128 * 16 * 128)
        sm = smalls.copy()
        sm[0:8, 8] = dtb[js]
        sm[0:8, 9] = alog[js]
        sm[0:128, 10] = normw

        in_maps.append({
            "xt": xt, "xq": xq, "xp2": xp2, "wbig": wbig, "convw": cw,
            "cf": cf, "h64": np.ascontiguousarray(h64, np.float32), "smalls": sm,
            "sel8": np.ascontiguousarray(sel8, np.float32),
            "woutT": np.ascontiguousarray(woutT, np.float32),
            "w13T": np.ascontiguousarray(w13T, np.float32),
            "w2T": np.ascontiguousarray(W2.T, np.float32),
            "rk": np.array([[g]], np.uint32),
        })
    return in_maps


def build(cfg):
    nc = bass.Bass("TRN2", target_bir_lowering=False, num_devices=8,
                   detect_race_conditions=False)
    sc = Sched()
    T, TT, C, CPT = cfg.T, cfg.TT, cfg.C, cfg.CPT
    T4, T16, P2TT = cfg.T4, cfg.T16, cfg.P2TT
    KCH, NCOL, NOC = cfg.KCH, cfg.NCOL, cfg.NOC

    DI = lambda n, s, dt=F32: nc.dram_tensor(n, s, dt, kind="ExternalInput").ap()
    d_xt = DI("xt", [128, KCH * T])
    d_xq = DI("xq", [128, KCH * T4])
    d_xp2 = DI("xp2", [128, KCH * 4 * T16])
    d_wbig = DI("wbig", [128, KCH * NCOL])
    d_convw = DI("convw", [128, 64])
    d_cf = DI("cf", [128, 7 * 128 + 1024])
    d_h64 = DI("h64", [64, 64])
    d_smalls = DI("smalls", [128, 24])
    d_sel8 = DI("sel8", [1, 32])
    d_woutT = DI("woutT", [128, 16 * 32 * 128])
    d_w13T = DI("w13T", [128, 128 * 16 * 128])
    d_w2T = DI("w2T", [8192, 2048])
    d_rk = DI("rk", [1, 1], U32)

    d_out = nc.dram_tensor("yout", [128, KCH * 4 * T16], F32, kind="ExternalOutput").ap()
    dbg = {}
    if cfg.debug:
        dbg["qkvc"] = nc.dram_tensor("dbg_qkvc", [128, 16 * T], BF16, kind="ExternalOutput").ap()
        dbg["gout"] = nc.dram_tensor("dbg_gout", [128, 8 * T], BF16, kind="ExternalOutput").ap()

    d_sti = nc.dram_tensor("stats_in", [1, 2 * T4], BF16, kind="Internal").ap()
    d_stg = nc.dram_tensor("stats_g", [1, 8 * T4], BF16, kind="Internal").ap()
    d_gq = [nc.dram_tensor(f"gq{q}", [1, 1024 * T4], BF16, kind="Internal").ap() for q in range(4)]
    d_gg = [nc.dram_tensor(f"gg{q}", [1, 4096 * T4], BF16, kind="Internal").ap() for q in range(4)]
    d_gsl = [nc.dram_tensor(f"gsl{q}", [1, 4096 * T16], BF16, kind="Internal").ap() for q in range(4)]
    d_w2t = nc.dram_tensor("w2tilde", [128, 16 * 64 * 128], BF16, kind="Internal").ap()
    d_wobf = nc.dram_tensor("woutbf", [128, 16 * 32 * 128], BF16, kind="Internal").ap()
    d_w13bf = nc.dram_tensor("w13bf", [128, 128 * 16 * 128], BF16, kind="Internal").ap()

    SD = sc.sem("sd"); GD = sc.sem("gd"); S2 = sc.sem("s2")
    TE = sc.sem("te"); DV = sc.sem("dv"); SE = sc.sem("se"); GC = sc.sem("gc")
    ST = sc.sem("st"); CCs = sc.sem("cc")
    LD1 = sc.sem("ld1")
    G0s = sc.sem("g0s")
    GX = [sc.sem("gx0"), sc.sem("gx1")]
    MR = [sc.sem("mr0"), sc.sem("mr1")]
    SB1 = sc.sem("sb1")
    SS = sc.sem("ss")
    WPl = sc.sem("wpl")
    WPs = sc.sem("wps")
    RS = [sc.sem(f"rs{q}") for q in range(4)]
    G2S = sc.sem("g2s")
    WOS = [sc.sem("wos0"), sc.sem("wos1")]
    W13S = [sc.sem(f"w13s{i}") for i in range(4)]
    W2TS = [sc.sem("w2ts0"), sc.sem("w2ts1")]
    X2S = [sc.sem("x2s0"), sc.sem("x2s1")]

    A = lambda n, s, dt: nc.alloc_sbuf_tensor(n, s, dt)
    p1g = []
    def A1(n, s, dt):
        g = nc.sbuf_tensor(n, s, dt)
        h = g.__enter__()
        p1g.append(g)
        return h
    c_ident = A("c_ident", [128, 128], BF16)
    c_h128 = A("c_h128", [128, 128], BF16)
    c_small = A("c_small", [128, 24], F32)
    c_ones_col = A("c_ones_col", [128, 1], BF16)
    c_ones_rowF = A("c_ones_rowF", [1, 128], F32)
    c_ones_row = A("c_ones_row", [1, 128], BF16)
    b_rkrg = A("b_rkrg", [1, 1], U32)
    c_bd64 = A1("c_bd64", [128, 128], BF16)
    c_offn = A1("c_offn", [128, 128], BF16)
    c_addLow = A1("c_addLow", [128, 128], F32)
    c_addUpS = A1("c_addUpS", [128, 128], F32)
    c_addUpI = A1("c_addUpI", [128, 128], F32)
    c_h64 = A1("c_h64", [64, 64], BF16)
    c_ej = A1("c_ej", [8, 1024], F32)
    c_ejb = A1("c_ejb", [8, 1024], BF16)
    c_ejcol = A1("c_ejcol", [8, 8], F32)
    c_sel8 = A1("c_sel8", [1, 32], F32)
    c_expAn = A1("c_expAn", [8, 1], F32)
    c_negW1 = A1("c_negW1", [1, NOC * 128], BF16)
    b_cw = A1("b_cw", [128, 64], F32)

    p_acc = [nc.alloc_psum_tensor("p_acc0", [128, 512], F32),
             nc.alloc_psum_tensor("p_acc1", [128, 512], F32)]
    p_kq = nc.alloc_psum_tensor("p_kq", [128, 512], F32)
    p_ap = nc.alloc_psum_tensor("p_ap", [128, 512], F32)
    p_sq = nc.alloc_psum_tensor("p_sq", [128, 512], F32)
    p_sq2 = nc.alloc_psum_tensor("p_sq2", [128, 512], F32)
    p_uso = nc.alloc_psum_tensor("p_uso", [128, 512], F32)
    p_bf = nc.alloc_psum_tensor("p_bf", [128, 1024], BF16)

    cons = {}

    def MM(region, out, lhsT, rhs, start=True, stop=True, wait=(), inc=()):
        w = list(wait)
        if start and region in cons:
            w.append(cons[region])
        sc.emit("tensor", lambda t, o=out, a=lhsT, b=rhs, s1=start, s2=stop:
                t.matmul(o, a, b, start=s1, stop=s2), wait=w, inc=inc)

    def TRP(region, out, in_, wait=(), inc=()):
        w = list(wait)
        if region in cons:
            w.append(cons[region])
        sc.emit("tensor", lambda t, o=out, a=in_: t.transpose(o, a, c_ident[:]),
                wait=w, inc=inc)

    def tmark():
        sc.emit("tensor", lambda t: t.nop(), inc=[(TE, 1)])
        return (TE, TE.n)

    def consume(region, mk):
        cons[region] = mk

    def EV(fn, wait=()):
        sc.emit("vector", fn, wait=wait, inc=[(DV, 1)])
        return (DV, DV.n)

    def ES(fn, wait=()):
        sc.emit("scalar", fn, wait=wait, inc=[(SE, 1)])
        return (SE, SE.n)

    def EG(fn, wait=()):
        sc.emit("gpsimd", fn, wait=wait, inc=[(GC, 1)])
        return (GC, GC.n)

    def TEi(fn, wait=()):
        sc.emit("tensor", fn, wait=wait, inc=[(TE, 1)])
        return (TE, TE.n)

    sc.emit("sync", lambda s: s.dma_start(out=c_small[:], in_=d_smalls), inc=[(SD, 16)])
    sc.emit("sync", lambda s: s.dma_start(out=c_ejcol[:], in_=d_smalls[0:8, 0:8]), inc=[(SD, 16)])
    sc.emit("sync", lambda s: s.dma_start(out=c_sel8[:], in_=d_sel8), inc=[(SD, 16)])
    sc.emit("sync", lambda s: s.dma_start(out=b_cw[:], in_=d_convw), inc=[(SD, 16)])
    sc.emit("sync", lambda s: s.dma_start(out=b_rkrg[:], in_=d_rk), inc=[(SD, 16)])
    rkldv = (SD, SD.n)
    sc.emit("sync", lambda s: s.dma_start(out=c_addLow[:], in_=d_cf[:, 384:512]), inc=[(SD, 16)])
    sc.emit("sync", lambda s: s.dma_start(out=c_addUpS[:], in_=d_cf[:, 512:640]), inc=[(SD, 16)])
    sc.emit("sync", lambda s: s.dma_start(out=c_addUpI[:], in_=d_cf[:, 640:768]), inc=[(SD, 16)])
    sc.emit("sync", lambda s: s.dma_start(out=c_ej[:], in_=d_cf[0:8, 896:1920]), inc=[(SD, 16)])
    for dst, src in [(c_ident, d_cf[:, 0:128]), (c_bd64, d_cf[:, 128:256]),
                     (c_offn, d_cf[:, 256:384]), (c_h128, d_cf[:, 768:896]),
                     (c_h64, d_h64), (c_ejb, d_cf[0:8, 896:1920])]:
        sc.emit("gpsimd", lambda g, d=dst, s=src: g.dma_start(out=d[:], in_=s), inc=[(LD1, 16)])
    sc.emit("gpsimd", lambda g: g.dma_start(out=d_wobf, in_=d_woutT), inc=[(LD1, 16)])
    sc.emit("gpsimd", lambda g: g.dma_start(out=d_w13bf, in_=d_w13T), inc=[(LD1, 16)])
    sc.emit("gpsimd", lambda g: g.memset(c_ones_col[:], 1.0))
    sc.emit("gpsimd", lambda g: g.memset(c_ones_row[:], 1.0))
    sc.emit("gpsimd", lambda g: g.memset(c_ones_rowF[:], 1.0), inc=[(GC, 1)])
    ld0 = [(SD, SD.n), (LD1, LD1.n), (GC, GC.n)]
    rcell = {}

    def _ldrk(s):
        rk = s.value_load(b_rkrg[0:1, 0:1])
        for m in range(4):
            rcell[m] = s.snap(rk + 4 * m)
        return None
    sc.emit("sync", _ldrk, wait=[rkldv])
    sA = ES(lambda s: s.activation(c_expAn[:], c_small[0:8, 9:10], AF.Exp), wait=ld0)
    EV(lambda v: v.tensor_scalar(c_expAn[:], c_expAn[:], -1.0, None, ALU.mult), wait=[sA])

    b_w = A1("b_w", [128, KCH * NCOL], BF16)
    b_x = [A1(f"b_x{i}", [128, KCH * TT], BF16) for i in range(2)]
    b_qkvc = [A1(f"b_qkvc{i}", [128, 16 * TT], BF16) for i in range(2)]
    b_zsil = [A1(f"b_zsil{i}", [128, 8 * TT], BF16) for i in range(2)]
    b_raw = [A1(f"b_raw{i}", [128, TT + 3], F32) for i in range(2)]
    b_zscr = A1("b_zscr", [128, TT], F32)
    b_zscr2 = A1("b_zscr2", [128, TT], F32)
    b_tail = A1("b_tail", [128, 16 * 3], F32)
    _sq1 = A1("b_sq", [128, TT], BF16)
    b_sq = [_sq1, _sq1]
    b_mrow = [A1(f"b_mrow{i}", [1, TT], BF16) for i in range(2)]
    b_rstdb = [A1(f"b_rstdb{i}", [128, TT], F32) for i in range(2)]
    b_gst = A1("b_gst", [128, 8 * TT], BF16)
    ROWSP = ["lnb", "sk3a", "lendf", "colA", "rowA", "rowO"]
    ROWSS = ["lg", "t0", "L", "Lrel"]
    b_rows = [A1(f"b_rows{i}", [8, len(ROWSP) * TT], F32) for i in range(2)]
    b_rowsS = A1("b_rowsS", [8, len(ROWSS) * TT], F32)
    RO = {n: i * TT for i, n in enumerate(ROWSP)}
    ROS = {n: i * TT for i, n in enumerate(ROWSS)}
    b_sqr = [A1(f"b_sqr{i}", [8, TT], BF16) for i in range(2)]
    b_carry = A1("b_carry", [8, 1], F32)
    _lnrp1 = A1("b_lnrp", [1, 4 * TT], F32)
    b_lnrp = [_lnrp1, _lnrp1]
    _lnrkJ1 = A1("b_lnrkJ", [8, TT], F32)
    b_lnrkJ = [_lnrkJ1, _lnrkJ1]
    _lnrqJ1 = A1("b_lnrqJ", [8, TT], F32)
    b_lnrqJ = [_lnrqJ1, _lnrqJ1]
    b_S = [A1(f"b_S{j}", [128, 128], F32) for j in range(8)]
    b_Sbf = [A1(f"b_Sbf{j}", [128, 128], BF16) for j in range(8)]
    Wk = {}
    for n in ["kk", "kqt", "ktm", "vtm", "expa", "expat", "expot", "kkbd", "kkoff",
              "abd", "atbd", "etneg", "mot", "k3", "qh",
              "n2", "n2t", "n4", "n4t", "n8", "n8t", "n16", "n16t", "n32t",
              "cg1", "cg2", "u", "wt", "osq", "sigbc"]:
        Wk[n] = A1("s_" + n, [128, 128], BF16)
    b_arg = A1("s_arg", [128, 128], F32)
    b_R = A1("s_R", [128, 256], BF16)
    b_Z = A1("s_Z", [128, 256], BF16)
    b_Y0 = b_Z
    b_X = b_R
    b_scols = A1("s_cols", [128, 5], F32)
    b_scolsE = A1("s_colsE", [128, 5], F32)
    b_sr1 = A1("s_sr1", [1, 128], F32)
    b_sbfr = A1("s_sbfr", [1, 128], BF16)
    b_g1 = b_arg
    b_strow = A1("b_strow", [1, 8], F32)
    b_stbf = A1("b_stbf", [1, 2 * TT], BF16)
    assert 8 * TT >= 2048
    b_w2pre = b_gst[0:64, 0:2048]
    b_w2ps = b_gst[64:128, 0:2048]

    sc.emit("gpsimd", lambda g: g.memset(b_tail[:], 0.0))
    sc.emit("gpsimd", lambda g: g.memset(b_carry[:], 0.0))
    for j in range(8):
        sc.emit("gpsimd", lambda g, j=j: g.memset(b_S[j][:], 0.0))
        sc.emit("gpsimd", lambda g, j=j: g.memset(b_Sbf[j][:], 0.0))
    sc.emit("gpsimd", lambda g: g.nop(), inc=[(GC, 1)])
    zinit = (GC, GC.n)

    sc.emit("gpsimd", lambda g: g.dma_start(out=b_w[:], in_=d_wbig), inc=[(LD1, 16)])
    wv1 = (LD1, LD1.n)
    for oc in range(NOC):
        for kc in range(KCH):
            MM("kq", p_kq[0:1, (oc % 4) * 128:(oc % 4) * 128 + 128],
               c_ones_col[:], b_w[:, kc * NCOL + oc * 128: kc * NCOL + oc * 128 + 128],
               start=(kc == 0), stop=(kc == KCH - 1),
               wait=([wv1] if (oc == 0 and kc == 0) else ()),
               inc=([(TE, 1)] if kc == KCH - 1 else ()))
        if oc % 4 == 3 or oc == NOC - 1:
            tev = (TE, TE.n)
            lo = (oc // 4) * 4
            v = EV(lambda v, lo=lo, oc=oc: v.tensor_scalar(
                c_negW1[0:1, lo * 128:(oc + 1) * 128],
                p_kq[0:1, 0:(oc + 1 - lo) * 128], -1.0, None, ALU.mult), wait=[tev])
            consume("kq", v)

    w2T_v = d_w2T.rearrange("(a b) n -> b a n", b=128)
    w2t_wv = d_w2t.rearrange("p (a n) -> p a n", a=64)
    for bb in range(128):
        sc.emit("gpsimd", lambda g, bb=bb: g.dma_start(out=b_w2pre, in_=w2T_v[bb]),
                wait=[cons.get("w2pre")], inc=[(WPl, 16)])
        gdv = (WPl, WPl.n)
        for nn in range(4):
            MM("ap", p_ap[0:64, 0:512], c_h64[:], b_w2pre[:, nn * 512:(nn + 1) * 512],
               wait=([gdv, cons.get("w2ps")] if nn == 0 else [cons.get("w2ps")]),
               inc=[(TE, 1)])
            tev = (TE, TE.n)
            v = EV(lambda v, nn=nn: v.tensor_scalar(
                b_w2ps[:, nn * 512:(nn + 1) * 512], p_ap[0:64, 0:512],
                c_small[0:64, 11:12], None, ALU.mult), wait=[tev])
            consume("ap", v)
        cons["w2pre"] = (TE, TE.n)
        dvv = (DV, DV.n)
        sc.emit("sync", lambda s, bb=bb: s.dma_start(
            out=d_w2t[bb:bb + 1, :].rearrange("p (oc a n) -> (p a) oc n", oc=16, a=64),
            in_=b_w2ps.rearrange("a (oc n) -> a oc n", oc=16)),
                wait=[dvv], inc=[(WPs, 16)])
        cons["w2ps"] = (WPs, WPs.n)
    w2prep_done = (WPs, WPs.n)
    cons["gst"] = w2prep_done

    sc.mark("p0")
    NTQ = T4 // TT
    xq_v = d_xq.rearrange("p (k t) -> p k t", k=KCH)
    for t in range(NTQ):
        sc.emit("gpsimd", lambda g, t=t: g.dma_start(
            out=b_x[0][:].rearrange("p (k t) -> p k t", k=KCH),
            in_=xq_v[:, :, t * TT:(t + 1) * TT]),
            wait=[cons.get("x0b"), (ld0[1] if t == 0 else None)], inc=[(G0s, 16)])
        ldv = (G0s, G0s.n)
        for kc in range(KCH):
            MM("kq", p_kq[0:1, 0:TT], c_ones_col[:], b_x[0][:, kc * TT:(kc + 1) * TT],
               start=(kc == 0), stop=(kc == KCH - 1),
               wait=([ldv] if kc == 0 else ()), inc=([(TE, 1)] if kc == KCH - 1 else ()))
        for kc in range(KCH):
            sev = ES(lambda s, kc=kc: s.activation(
                b_sq[0][:, 0:TT], b_x[0][:, kc * TT:(kc + 1) * TT], AF.Square),
                wait=([ldv] if kc == 0 else [(TE, TE.n)]))
            MM("ap", p_ap[0:1, 0:TT], c_ones_col[:], b_sq[0][:, 0:TT],
               start=(kc == 0), stop=(kc == KCH - 1), wait=[sev], inc=[(TE, 1)])
        cons["x0b"] = (TE, TE.n)
        t1 = (TE, TE.n)
        v1 = EV(lambda v: v.tensor_scalar(_lnrp1[0:1, 0:TT], p_kq[0:1, 0:TT],
                                          c_small[0:1, 16:17], None, ALU.mult), wait=[t1])
        consume("kq", v1)
        v2 = EV(lambda v: v.scalar_tensor_tensor(
            _lnrp1[0:1, TT:2 * TT], _lnrp1[0:1, 0:TT], -1.0,
            _lnrp1[0:1, 0:TT], ALU.mult, ALU.mult), wait=[v1])
        v3 = EV(lambda v: v.scalar_tensor_tensor(
            _lnrp1[0:1, TT:2 * TT], p_ap[0:1, 0:TT], 1.0 / 2048.0,
            _lnrp1[0:1, TT:2 * TT], ALU.mult, ALU.add), wait=[v2])
        consume("ap", v3)
        s1 = ES(lambda s: s.activation(_lnrp1[0:1, TT:2 * TT], _lnrp1[0:1, TT:2 * TT],
                                       AF.Ln, bias=c_small[0:1, 15:16]), wait=[v3])
        s2 = ES(lambda s: s.activation(b_stbf[0:1, TT:2 * TT], _lnrp1[0:1, TT:2 * TT],
                                       AF.Exp, scale=c_small[0:1, 14:15]), wait=[s1])
        s3 = ES(lambda s: s.activation(b_stbf[0:1, 0:TT], _lnrp1[0:1, 0:TT], AF.Copy),
                wait=[v3])
        sc.emit("sync", lambda s, t=t: s.dma_start(
            out=d_sti[0:1, t * TT:(t + 1) * TT], in_=b_stbf[0:1, 0:TT]),
            wait=[s3], inc=[(SS, 16)])
        sc.emit("sync", lambda s, t=t: s.dma_start(
            out=d_sti[0:1, T4 + t * TT:T4 + (t + 1) * TT], in_=b_stbf[0:1, TT:2 * TT]),
            wait=[s2], inc=[(SS, 16)])
        cons["stbf"] = (SS, SS.n)
        cons["strow"] = (SS, SS.n)
    stv = (SS, SS.n)
    cons["lnrp"] = stv
    p0_x_done = (TE, TE.n)
    sc.emit("gpsimd", lambda g: g.collective_compute(
        "AllGather", ALU.bypass, replica_groups=[[0, 1, 2, 3], [4, 5, 6, 7]],
        ins=[d_sti], outs=[d_stg]), wait=[stv], inc=[(CCs, 1)])
    agst = (CCs, CCs.n)

    xt_v = d_xt.rearrange("p (k t) -> p k t", k=KCH)

    state = {"prev_head_te": None, "Scons": [zinit] * 8,
             ("rows_done", 0): zinit, ("rows_done", 1): zinit,
             ("sqv", 0): zinit, ("sqv", 1): zinit}

    def scan_head(t, par, cc_, cs, h, jj, ksl, qsl, h_done):
        pht = state["prev_head_te"]
        blocks = ["rowA", "colA", "lnb", "sk3a", "lendf"]
        for i, blk in enumerate(blocks):
            MM("kq", p_kq[:, 256 + i:257 + i],
               b_rows[par][0:8, RO[blk] + cs:RO[blk] + cs + C],
               c_ejcol[0:8, jj:jj + 1],
               wait=([state[("rows_done", par)]] if i == 0 else ()),
               inc=([(TE, 1)] if i == len(blocks) - 1 else ()))
        tev = (TE, TE.n)
        ecols = EV(lambda v: v.tensor_copy(b_scols[:], p_kq[:, 256:261]),
                   wait=[tev, pht])
        consume("kq", ecols)
        sexp = ES(lambda s: s.activation(b_scolsE[:], b_scols[:], AF.Exp), wait=[ecols])
        r3 = b_rows[par][0:8, :].rearrange("p (b t) -> p b t", t=TT)[:, 3:6, cs:cs + C]
        MM("uso", p_uso[:, 0:384], c_ej[0:8, jj * 128:(jj + 1) * 128], r3,
           wait=[state[("rows_done", par)]], inc=[(TE, 1)])
        bcv = (TE, TE.n)
        a1 = EV(lambda v: v.scalar_tensor_tensor(
            b_arg[:], p_uso[:, 0:128], b_scols[:, 0:1], c_addLow[:], ALU.add, ALU.add),
            wait=[bcv, sexp])
        x1 = ES(lambda s: s.activation(Wk["expa"][:], b_arg[:], AF.Exp), wait=[a1])
        a2 = EV(lambda v: v.scalar_tensor_tensor(
            b_arg[:], p_uso[:, 128:256], b_scols[:, 1:2], c_addUpS[:], ALU.add, ALU.add),
            wait=[x1])
        x2 = ES(lambda s: s.activation(Wk["expat"][:], b_arg[:], AF.Exp), wait=[a2])
        a3 = EV(lambda v: v.scalar_tensor_tensor(
            b_arg[:], p_uso[:, 256:384], b_scols[:, 1:2], c_addUpI[:], ALU.add, ALU.add),
            wait=[x2])
        x3 = ES(lambda s: s.activation(Wk["expot"][:], b_arg[:], AF.Exp), wait=[a3])
        consume("uso", a3)
        m1 = EV(lambda v: v.tensor_tensor(Wk["abd"][:], Wk["expa"][:], Wk["kkbd"][:],
                                          ALU.mult), wait=[x1] + list(h_done[3:4]))
        m2 = EV(lambda v: v.tensor_tensor(Wk["atbd"][:], Wk["expat"][:], Wk["kkbd"][:],
                                          ALU.mult), wait=[x2])
        m3 = EV(lambda v: v.tensor_tensor(Wk["etneg"][:], Wk["expat"][:], Wk["kkoff"][:],
                                          ALU.mult), wait=[x2] + list(h_done[4:5]))
        m4 = EV(lambda v: v.tensor_tensor(Wk["mot"][:], Wk["expot"][:], Wk["kqt"][:],
                                          ALU.mult), wait=[x3] + list(h_done[2:3]))
        vsl = b_qkvc[par][:, (8 + jj) * TT + cs:(8 + jj) * TT + cs + C]
        TRP("bf", p_bf[:, 128:256], vsl, wait=[cons.get(f"qkvc{par}_{8 + jj}")], inc=[(TE, 1)])
        tev = (TE, TE.n)
        ev = EV(lambda v: v.tensor_copy(Wk["vtm"][:], p_bf[:, 128:256]), wait=[tev])
        consume("bf", ev)
        r1 = EV(lambda v: v.tensor_scalar(b_R[:, 0:128], Wk["ktm"][:],
                                          b_scolsE[:, 0:1], None, ALU.mult),
                wait=[sexp] + list(h_done[0:1]))
        r2 = EV(lambda v: v.tensor_scalar(b_R[:, 128:256], Wk["vtm"][:],
                                          b_scolsE[:, 2:3], None, ALU.mult), wait=[ev])
        r3v = EV(lambda v: v.tensor_scalar(Wk["k3"][:], Wk["ktm"][:],
                                           b_scolsE[:, 3:4], None, ALU.mult))
        MM("uso", p_uso[:, 384:512], c_ejb[0:8, jj * 128:(jj + 1) * 128],
           b_sqr[par][0:8, cs:cs + C], wait=[state[("sqv", par)]], inc=[(TE, 1)])
        tev = (TE, TE.n)
        qv = EV(lambda v: v.tensor_tensor(Wk["qh"][:], qsl, p_uso[:, 384:512],
                                          ALU.mult), wait=[tev, cons.get(f"qkvc{par}_{h}")])
        consume("uso", qv)
        pw = [("n2", "n2t", "abd", "atbd"), ("n4", "n4t", "n2t", "n2"),
              ("n8", "n8t", "n4t", "n4"), ("n16", "n16t", "n8t", "n8"),
              ("n32", "n32t", "n16t", "n16")]
        mwait = [m1, m2]
        for si, (pa_, pb_, la, lb) in enumerate(pw):
            last = si == len(pw) - 1
            if not last:
                MM("sq", p_sq[:, 0:128], Wk[la][:], Wk[lb][:], wait=mwait, inc=[(TE, 1)])
                t_a = (TE, TE.n)
            MM("sq2", p_sq2[:, 0:128], Wk[lb][:], Wk[la][:],
               wait=(mwait if last else ()), inc=[(TE, 1)])
            t_b = (TE, TE.n)
            if not last:
                e_a = EV(lambda v, pa_=pa_: v.tensor_copy(Wk[pa_][:], p_sq[:, 0:128]),
                         wait=[t_a])
                consume("sq", e_a)
            e_b = EV(lambda v, pb_=pb_: v.tensor_copy(Wk[pb_][:], p_sq2[:, 0:128]),
                     wait=[t_b])
            consume("sq2", e_b)
            mwait = ([e_a, e_b] if not last else [e_b])
        cgA, cgB = "cg1", "cg2"
        ch = EV(lambda v, cgA=cgA: v.scalar_tensor_tensor(
            Wk[cgA][:], Wk["atbd"][:], -1.0, c_ident[:], ALU.mult, ALU.add),
                wait=[m2])
        for Pt in ("n2t", "n4t", "n8t", "n16t", "n32t"):
            MM("ap", p_ap[:, 0:128], Wk[Pt][:], Wk[cgA][:], wait=[ch], inc=[(TE, 1)])
            tev = (TE, TE.n)
            ch = EV(lambda v, cgA=cgA, cgB=cgB: v.tensor_tensor(
                Wk[cgB][:], p_ap[:, 0:128], Wk[cgA][:], ALU.add), wait=[tev])
            consume("ap", ch)
            cgA, cgB = cgB, cgA
        Tf = cgA
        MM("ap", p_ap[:, 0:256], Wk[Tf][:], b_R[:], wait=[ch, r1, r2], inc=[(TE, 1)])
        tev = (TE, TE.n)
        ey = EV(lambda v: v.tensor_copy(b_Y0[:], p_ap[:, 0:256]), wait=[tev])
        consume("ap", ey)
        MM("ap", p_ap[:, 0:256], Wk["etneg"][:], b_Y0[:], wait=[m3, ey], inc=[(TE, 1)])
        tev = (TE, TE.n)
        ez = EV(lambda v: v.tensor_tensor(b_Z[:], p_ap[:, 0:256], b_R[:], ALU.add),
                wait=[tev])
        consume("ap", ez)
        MM("ap", p_ap[:, 0:256], Wk[Tf][:], b_Z[:], wait=[ez], inc=[(TE, 1)])
        tev = (TE, TE.n)
        ex = EV(lambda v: v.tensor_copy(b_X[:], p_ap[:, 0:256]), wait=[tev])
        consume("ap", ex)
        TRP("bf", p_bf[:, 256:384], b_X[:, 0:128], wait=[ex], inc=[(TE, 1)])
        tev = (TE, TE.n)
        ew = EV(lambda v: v.tensor_copy(Wk["wt"][:], p_bf[:, 256:384]), wait=[tev])
        consume("bf", ew)
        MM("sq", p_sq[:, 0:128], Wk["wt"][:], b_Sbf[jj][:],
           wait=[ew, state["Scons"][jj]], inc=[(TE, 1)])
        tev = (TE, TE.n)
        eu = EV(lambda v: v.scalar_tensor_tensor(Wk["u"][:], p_sq[:, 0:128], -1.0,
                                                 b_X[:, 128:256], ALU.mult, ALU.add),
                wait=[tev])
        consume("sq", eu)
        MM("uso", p_uso[:, 0:128], b_Sbf[jj][:], Wk["qh"][:], stop=False,
           wait=[qv], inc=())
        MM("uso", p_uso[:, 0:128], Wk["u"][:], Wk["mot"][:], start=False,
           wait=[eu, m4], inc=[(TE, 1)])
        tO = (TE, TE.n)
        MM("sq2", p_sq2[:, 0:128], Wk["k3"][:], Wk["u"][:], wait=[r3v], inc=[(TE, 1)])
        tev = (TE, TE.n)
        es1 = EV(lambda v: v.scalar_tensor_tensor(
            b_S[jj][:], b_S[jj][:], b_scolsE[:, 4:5], p_sq2[:, 0:128],
            ALU.mult, ALU.add), wait=[tev])
        consume("sq2", es1)
        es2 = EV(lambda v: v.tensor_copy(b_Sbf[jj][:], b_S[jj][:]), wait=[es1])
        state["Scons"][jj] = es2
        so = ES(lambda s: s.activation(Wk["osq"][:], p_uso[:, 0:128], AF.Square),
                wait=[tO])
        MM("uso2", p_uso[0:1, 128:256], c_ones_col[:], Wk["osq"][:], wait=[so],
           inc=[(TE, 1)])
        tev = (TE, TE.n)
        sr = ES(lambda s: s.activation(b_sr1[:], p_uso[0:1, 128:256], AF.Ln,
                                       bias=c_small[0:1, 12:13],
                                       scale=c_small[0:1, 20:21]), wait=[tev])
        sr2 = ES(lambda s: s.activation(b_sbfr[:], b_sr1[:], AF.Exp,
                                        scale=c_small[0:1, 14:15]), wait=[sr])
        MM("uso3", p_uso[:, 256:384], c_ones_row[:], b_sbfr[:], wait=[sr2],
           inc=[(TE, 1)])
        tev = (TE, TE.n)
        eb = EV(lambda v: v.tensor_copy(Wk["sigbc"][:], p_uso[:, 256:384]), wait=[tev])
        g1 = EV(lambda v: v.scalar_tensor_tensor(
            b_g1[:], p_uso[:, 0:128], c_small[:, 10:11], Wk["sigbc"][:],
            ALU.mult, ALU.mult), wait=[eb])
        gv = EV(lambda v, jj=jj, cs=cs: v.tensor_tensor(
            b_gst[:, jj * TT + cs:jj * TT + cs + C], b_g1[:],
            b_zsil[par][:, jj * TT + cs:jj * TT + cs + C], ALU.mult),
            wait=[g1, cons.get(f"zsil{par}{jj}"), cons.get("gst")])
        consume("uso", gv)
        cons["uso2"] = gv
        cons["uso3"] = gv
        state["prev_head_te"] = (TE, TE.n)

    x_cons = [p0_x_done, None]
    zs_cons = [[None] * 8, [None] * 8]
    ms_cons = [None, None]
    rb_cons = [None, None]
    qkvc_cons = {}
    gst_cons = None
    sQr_c = {0: None, 1: None}
    gq_st = [0, 0, 0, 0]

    def R(par, n, cs=None, ce=None):
        if n in ROS:
            buf, o = b_rowsS, ROS[n]
        else:
            buf, o = b_rows[par], RO[n]
        if cs is None:
            return buf[0:8, o:o + TT]
        return buf[0:8, o + cs:o + ce]

    def prep_a(t):
        par = t % 2
        q = t // cfg.TPQ
        tq = t % cfg.TPQ
        sc.emit("gpsimd", lambda g, t=t, par=par: g.dma_start(
            out=b_x[par][:].rearrange("p (k t) -> p k t", k=KCH),
            in_=xt_v[:, :, t * TT:(t + 1) * TT]),
            wait=[x_cons[par]], inc=[(GX[par], 16)])
        xv = (GX[par], GX[par].n)
        mofs = q * 2 * T4 + tq * TT
        rofs = q * 2 * T4 + T4 + tq * TT
        sc.emit("sync", lambda s, mofs=mofs, par=par: s.dma_start(
            out=b_mrow[par][:], in_=d_stg[0:1, mofs:mofs + TT]),
            wait=[agst, ms_cons[par]], inc=[(MR[par], 16)])
        mrv = (MR[par], MR[par].n)
        sc.emit("sync", lambda s, rofs=rofs: s.dma_start(
            out=b_stbf[0:1, 0:TT], in_=d_stg[0:1, rofs:rofs + TT]),
            wait=[agst, cons.get("stbf")], inc=[(SB1, 16)])
        mv = (SB1, SB1.n)
        MM("uso", p_uso[:, 0:TT], c_ones_row[:], b_stbf[0:1, 0:TT],
           wait=[mv], inc=[(TE, 1)])
        tev = (TE, TE.n)
        rv = EV(lambda v, par=par: v.tensor_copy(b_rstdb[par][:, 0:TT], p_uso[:, 0:TT]),
                wait=[tev, rb_cons[par]])
        consume("uso", rv)
        cons["stbf"] = rv
        rstdv = rv

        bgv = None
        for oc in range(NOC):
            pa = p_acc[oc % 2]
            reg = f"acc{oc % 2}"
            for kc in range(KCH):
                MM(reg, pa[:, 0:TT],
                   b_w[:, kc * NCOL + oc * 128: kc * NCOL + oc * 128 + 128],
                   b_x[par][:, kc * TT:(kc + 1) * TT],
                   start=(kc == 0), stop=False,
                   wait=([xv, rstdv] if kc == 0 else ()))
            MM(reg, pa[:, 0:TT], c_negW1[0:1, oc * 128:oc * 128 + 128],
               b_mrow[par][:], start=False, stop=True, wait=[mrv], inc=[(TE, 1)])
            mmv = (TE, TE.n)
            if oc < 16:
                rb = b_raw[oc % 2]
                tl = b_tail[:, oc * 3:oc * 3 + 3]
                rawc = cons.get(f"raw{oc % 2}")
                g0 = EG(lambda g, rb=rb, tl=tl: g.tensor_copy(rb[:, 0:3], tl),
                        wait=[rawc])
                ev = EV(lambda v, rb=rb, pa=pa, par=par: v.tensor_tensor(
                    rb[:, 3:TT + 3], pa[:, 0:TT], b_rstdb[par][:, 0:TT], ALU.mult),
                    wait=[mmv, g0])
                consume(reg, ev)
                g1 = EG(lambda g, rb=rb, tl=tl: g.tensor_copy(tl, rb[:, TT:TT + 3]),
                        wait=[ev])
                from concourse.bass import broadcast_tensor_aps as _bc

                def cvop(g, out, i0, w_col, add=None):
                    a0, a1 = _bc(i0, w_col)
                    if add is None:
                        return g.tensor_tensor(out, a0, a1, ALU.mult)
                    return None
                g2 = EG(lambda g, rb=rb, oc=oc: g.tensor_tensor(
                    b_zscr[:, 0:TT], *_bc(rb[:, 0:TT], b_cw[:, oc * 4:oc * 4 + 1]),
                    ALU.mult), wait=[ev, cons.get("zscr")])
                g3 = EG(lambda g, rb=rb, oc=oc: g.tensor_tensor(
                    b_zscr2[:, 0:TT], *_bc(rb[:, 1:TT + 1], b_cw[:, oc * 4 + 1:oc * 4 + 2]),
                    ALU.mult), wait=[cons.get("zscr2")])
                g3b = EG(lambda g: g.tensor_tensor(
                    b_zscr[:, 0:TT], b_zscr[:, 0:TT], b_zscr2[:, 0:TT], ALU.add))
                g4 = EG(lambda g, rb=rb, oc=oc: g.tensor_tensor(
                    b_zscr2[:, 0:TT], *_bc(rb[:, 2:TT + 2], b_cw[:, oc * 4 + 2:oc * 4 + 3]),
                    ALU.mult))
                g4b = EG(lambda g: g.tensor_tensor(
                    b_zscr[:, 0:TT], b_zscr[:, 0:TT], b_zscr2[:, 0:TT], ALU.add))
                g5a = EG(lambda g, rb=rb, oc=oc: g.tensor_tensor(
                    b_zscr2[:, 0:TT], *_bc(rb[:, 3:TT + 3], b_cw[:, oc * 4 + 3:oc * 4 + 4]),
                    ALU.mult))
                g5 = EG(lambda g, oc=oc: g.tensor_tensor(
                    b_qkvc[par][:, oc * TT:(oc + 1) * TT], b_zscr[:, 0:TT],
                    b_zscr2[:, 0:TT], ALU.add),
                    wait=[qkvc_cons.get((par, oc))])
                cons[f"raw{oc % 2}"] = g5
                cons["zscr"] = g5
                cons["zscr2"] = g5
                cons[f"qkvc{par}_{oc}"] = g5
            elif oc < 24:
                zi = oc - 16
                ev = EV(lambda v, pa=pa, par=par: v.tensor_tensor(
                    b_zscr[:, 0:TT], pa[:, 0:TT], b_rstdb[par][:, 0:TT], ALU.mult),
                    wait=[mmv, cons.get("zscr"), zs_cons[par][zi]])
                consume(reg, ev)
                sv = ES(lambda s, zi=zi, par=par: s.activation(
                    b_zsil[par][:, zi * TT:(zi + 1) * TT], b_zscr[:, 0:TT], AF.Silu),
                    wait=[ev])
                cons["zscr"] = sv
                cons[f"zsil{par}{zi}"] = sv
            else:
                v1 = EV(lambda v, pa=pa, par=par: v.tensor_tensor(
                    R(par, "lnb"), pa[0:8, 0:TT], b_rstdb[par][0:8, 0:TT], ALU.mult),
                    wait=[mmv, cons.get(f"rows{par}")])
                v2 = EV(lambda v, pa=pa, par=par: v.tensor_tensor(
                    R(par, "lg"), pa[32:40, 0:TT], b_rstdb[par][0:8, 0:TT], ALU.mult))
                consume(reg, v2)
                bgv = v2

        s1 = ES(lambda s: s.activation(R(par, "t0"), R(par, "lnb"), AF.Abs), wait=[bgv])
        s2 = ES(lambda s: s.activation(R(par, "t0"), R(par, "t0"), AF.Exp, scale=c_small[0:8, 21:22]))
        s3 = ES(lambda s: s.activation(R(par, "t0"), R(par, "t0"), AF.Ln, bias=c_small[0:8, 19:20]))
        s4 = ES(lambda s: s.activation(R(par, "lnb"), R(par, "lnb"), AF.Relu, scale=c_small[0:8, 21:22]))
        v1 = EV(lambda v: v.scalar_tensor_tensor(R(par, "lnb"), R(par, "lnb"), -1.0, R(par, "t0"),
                                                 ALU.mult, ALU.subtract), wait=[s4])
        s5 = ES(lambda s: s.activation(R(par, "t0"), R(par, "lg"), AF.Abs, bias=c_small[0:8, 8:9]),
                wait=[v1])
        s6 = ES(lambda s: s.activation(R(par, "t0"), R(par, "t0"), AF.Exp, scale=c_small[0:8, 21:22]))
        s7 = ES(lambda s: s.activation(R(par, "t0"), R(par, "t0"), AF.Ln, bias=c_small[0:8, 19:20]))
        s8 = ES(lambda s: s.activation(R(par, "lg"), R(par, "lg"), AF.Relu, bias=c_small[0:8, 8:9]))
        v2 = EV(lambda v: v.tensor_tensor(R(par, "lg"), R(par, "lg"), R(par, "t0"), ALU.add), wait=[s8])
        v3 = EV(lambda v: v.tensor_scalar(R(par, "lg"), R(par, "lg"), c_expAn[0:8, 0:1], None, ALU.mult))
        v4 = EV(lambda v: v.tensor_tensor_scan(R(par, "L"), R(par, "lg"), R(par, "lg"),
                                               b_carry[0:8, 0:1], ALU.add, ALU.bypass),
                wait=[v3, zinit])
        vv = v4
        for cc_ in range(CPT):
            cs, ce = cc_ * C, (cc_ + 1) * C
            if cc_ == 0:
                vv = EV(lambda v, cs=cs, ce=ce: v.tensor_scalar(
                    R(par, "Lrel", cs, ce), R(par, "L", cs, ce), b_carry[0:8, 0:1], None,
                    ALU.subtract), wait=[vv])
            else:
                vv = EV(lambda v, cs=cs, ce=ce: v.tensor_scalar(
                    R(par, "Lrel", cs, ce), R(par, "L", cs, ce),
                    b_rowsS[0:8, ROS["L"] + cs - 1:ROS["L"] + cs], None, ALU.subtract))
        v5 = EV(lambda v: v.tensor_copy(b_carry[0:8, 0:1],
                                        b_rowsS[0:8, ROS["L"] + TT - 1:ROS["L"] + TT]),
                wait=[vv])
        v6 = EV(lambda v: v.tensor_tensor(R(par, "lg"), R(par, "Lrel"), R(par, "lg"), ALU.subtract))
        for cc_ in range(CPT):
            cs, ce = cc_ * C, (cc_ + 1) * C
            EV(lambda v, cs=cs, ce=ce: v.tensor_scalar(
                R(par, "lendf", cs, ce), R(par, "Lrel", cs, ce), 0.0,
                b_rowsS[0:8, ROS["Lrel"] + ce - 1:ROS["Lrel"] + ce], ALU.mult, ALU.add))
        state[("pa_dv", par)] = (DV, DV.n)

    def prep_b(t):
        par = t % 2
        for w, dst in ((0, b_lnrkJ[par]), (1, b_lnrqJ[par])):
            for hh in range(4):
                h = 4 * w + hh
                oc = (4 + h) if h < 4 else (h - 4)
                sev = ES(lambda s, oc=oc: s.activation(
                    b_sq[par][:, 0:TT], b_qkvc[par][:, oc * TT:(oc + 1) * TT], AF.Square),
                    wait=[cons.get(f"qkvc{par}_{oc}"), (TE, TE.n) if h else None])
                MM("kq", p_kq[0:1, 0:TT], c_ones_col[:], b_sq[par][:, 0:TT], wait=[sev],
                   inc=[(TE, 1)])
                tev = (TE, TE.n)
                s9 = ES(lambda s, hh=hh, par=par: s.activation(
                    b_lnrp[par][0:1, hh * TT:(hh + 1) * TT], p_kq[0:1, 0:TT], AF.Ln,
                    bias=c_small[0:1, 13:14]), wait=[tev, cons.get("lnrp")])
                consume("kq", s9)
            lnv = (SE, SE.n)
            for hh in range(4):
                MM("uso", p_uso[0:8, 0:TT], c_sel8[0:1, hh * 8:hh * 8 + 8],
                   b_lnrp[par][0:1, hh * TT:(hh + 1) * TT],
                   start=(hh == 0), stop=(hh == 3),
                   wait=([lnv] if hh == 0 else ()), inc=([(TE, 1)] if hh == 3 else ()))
            tev = (TE, TE.n)
            cons["lnrp"] = tev
            v = EV(lambda v, dst=dst: v.tensor_copy(dst[0:8, 0:TT], p_uso[0:8, 0:TT]),
                   wait=[tev])
            consume("uso", v)
        va = EV(lambda v: v.tensor_tensor(R(par, "colA"), b_lnrkJ[par][0:8, 0:TT], R(par, "Lrel"),
                                          ALU.subtract))
        vb = EV(lambda v: v.tensor_tensor(R(par, "rowA"), R(par, "lnb"), R(par, "lg"), ALU.add))
        vc = EV(lambda v: v.tensor_tensor(R(par, "rowA"), R(par, "rowA"), b_lnrkJ[par][0:8, 0:TT], ALU.add))
        vd = EV(lambda v: v.tensor_tensor(R(par, "rowO"), R(par, "Lrel"), b_lnrqJ[par][0:8, 0:TT], ALU.add))
        ve = EV(lambda v: v.tensor_tensor(R(par, "sk3a"), R(par, "colA"), R(par, "lendf"), ALU.add))
        sqv = ES(lambda s: s.activation(b_sqr[par][0:8, 0:TT], R(par, "rowO"), AF.Exp),
                 wait=[vd, sQr_c[par]])
        state[("sqv", par)] = sqv
        rows_done = EV(lambda v: v.tensor_copy(b_strow[0:1, 0:1], b_rows[par][0:1, 0:1]),
                       wait=[va, vb, vc, vd, ve, state[("pa_dv", par)]])
        state[("rows_done", par)] = rows_done


    sc.mark("p1")
    prep_a(0)
    prep_b(0)
    for t in range(cfg.NT1):
        sc.mark(f"p1t{t}")
        par = t % 2
        q = t // cfg.TPQ
        tq = t % cfg.TPQ
        for cc_ in range(CPT):
            cs = cc_ * C
            for h in range(4):
                kofs = (4 + h) * TT + cs
                qofs = h * TT + cs
                ksl = b_qkvc[par][:, kofs:kofs + C]
                qsl = b_qkvc[par][:, qofs:qofs + C]
                TRP("bf", p_bf[:, 0:128], ksl,
                    wait=[cons.get(f"qkvc{par}_{4 + h}"), state[("rows_done", par)]],
                    inc=[(TE, 1)])
                tev = (TE, TE.n)
                e1 = EV(lambda v: v.tensor_copy(Wk["ktm"][:], p_bf[:, 0:128]),
                        wait=[tev])
                consume("bf", e1)
                MM("kq", p_kq[:, 0:128], ksl, ksl, inc=[(TE, 1)])
                MM("kq", p_kq[:, 128:256], ksl, qsl,
                   wait=[cons.get(f"qkvc{par}_{h}")], inc=[(TE, 1)])
                tev = (TE, TE.n)
                e2 = EV(lambda v: v.tensor_copy(Wk["kk"][:], p_kq[:, 0:128]),
                        wait=[tev, cons.get("kk")])
                e3 = EV(lambda v: v.tensor_copy(Wk["kqt"][:], p_kq[:, 128:256]),
                        wait=[cons.get("kqt")])
                consume("kq", e3)
                e4 = EV(lambda v: v.tensor_tensor(Wk["kkbd"][:], Wk["kk"][:],
                                                  c_bd64[:], ALU.mult),
                        wait=[cons.get("kkbd")])
                e5 = EV(lambda v: v.tensor_tensor(Wk["kkoff"][:], Wk["kk"][:],
                                                  c_offn[:], ALU.mult),
                        wait=[cons.get("kkoff")])
                h_done = (e1, e2, e3, e4, e5)

                for jj in (2 * h, 2 * h + 1):
                    scan_head(t, par, cc_, cs, h, jj, ksl, qsl, h_done)
                gi = cc_ * 4 + h
                if t + 1 < cfg.NT1:
                    if gi == 2:
                        prep_a(t + 1)
                    elif gi == 6:
                        prep_b(t + 1)

        scv = (TE, TE.n)
        sev = (SE, SE.n)
        dvv = (DV, DV.n)
        for oc in range(16):
            qkvc_cons[(par, oc)] = scv
        for zi in range(8):
            zs_cons[par][zi] = dvv
        x_cons[par] = scv
        ms_cons[par] = scv
        rb_cons[par] = dvv
        cons[f"rows{par}"] = scv
        sQr_c[par] = scv
        SEG = min(TT, T16)
        for jj in range(8):
            w = [dvv]
            for so in range(0, TT, SEG):
                sub = (tq * TT + so) // T16
                c0 = (tq * TT + so) % T16
                sc.emit("sync", lambda s, jj=jj, q=q, sub=sub, c0=c0, so=so: s.dma_start(
                    out=d_gq[q].rearrange("o (s r t) -> (o s) r t", s=4, t=T16)
                        [sub, jj * 128:(jj + 1) * 128, c0:c0 + SEG],
                    in_=b_gst[:, jj * TT + so:jj * TT + so + SEG]),
                    wait=w, inc=[(ST, 16)])
        gst_cons = (ST, ST.n)
        cons["gst"] = gst_cons
        if cfg.debug:
            for oc in range(16):
                sc.emit("sync", lambda s, oc=oc, t=t: s.dma_start(
                    out=dbg["qkvc"].rearrange("p (o t) -> p o t", o=16)[:, oc, t * TT:(t + 1) * TT],
                    in_=b_qkvc[par][:, oc * TT:(oc + 1) * TT]), wait=[scv], inc=[(ST, 16)])
            for jj in range(8):
                sc.emit("sync", lambda s, jj=jj, t=t: s.dma_start(
                    out=dbg["gout"].rearrange("p (o t) -> p o t", o=8)[:, jj, t * TT:(t + 1) * TT],
                    in_=b_gst[:, jj * TT:(jj + 1) * TT]), wait=[dvv], inc=[(ST, 16)])
            gst_cons = (ST, ST.n)
            cons["gst"] = gst_cons
        if tq == cfg.TPQ - 1:
            gq_st[q] = ST.n
            sc.emit("gpsimd", lambda g, q=q: g.collective_compute(
                "AllGather", ALU.bypass, replica_groups=[[0, 1, 2, 3], [4, 5, 6, 7]],
                ins=[d_gq[q]], outs=[d_gg[q]]),
                wait=[(ST, gq_st[q])], inc=[(CCs, 1)])
            agv = (CCs, CCs.n)
            CH = 1024 * T16
            for m in range(4):
                sc.emit("sync", lambda s, q=q, m=m, CH=CH: s.dma_start(
                    out=d_gsl[q][0:1, m * CH:(m + 1) * CH],
                    in_=d_gg[q][0:1, :][:, bass.ts(rcell[m], CH)]),
                    wait=[agv], inc=[(RS[q], 16)])
            gq_st[q] = RS[q].n

    ag_q = [(RS[q], gq_st[q]) for q in range(4)]

    sc.mark("barrier")
    for g in reversed(p1g):
        g.__exit__(None, None, None)
    snap = [(TE, TE.n), (DV, DV.n), (SE, SE.n), (GC, GC.n),
            (SD, SD.n), (GD, GD.n), (ST, ST.n), (S2, S2.n)]
    sc.emit("tensor", lambda t: t.nop(), wait=[snap[1], snap[2], snap[3]], inc=[(TE, 1)])
    sc.emit("vector", lambda v: v.nop(), wait=[snap[0], snap[2], snap[3]], inc=[(DV, 1)])
    sc.emit("scalar", lambda s: s.nop(), wait=[snap[0], snap[1], snap[3]], inc=[(SE, 1)])
    sc.emit("gpsimd", lambda g: g.nop(), wait=[snap[0], snap[1], snap[2]], inc=[(GC, 1)])
    sc.emit("sync", lambda s: s.nop(), wait=[snap[0], snap[1], snap[2], snap[3]])
    barrier = [(TE, TE.n), (DV, DV.n), (SE, SE.n), (GC, GC.n)]

    p2g = []
    def A2(n, s, dt):
        g = nc.sbuf_tensor(n, s, dt)
        h = g.__enter__()
        p2g.append(g)
        return h

    b_x1 = A2("p2_x1", [128, 16 * P2TT], F32)
    b_g2 = A2("p2_g2", [128, 32 * P2TT], BF16)
    b_midhi = A2("p2_midhi", [128, 32 * P2TT], BF16)
    def mids_ap(c):
        base = b_g2 if c < 32 else b_midhi
        return base[:, (c % 32) * P2TT:((c % 32) + 1) * P2TT]
    b_yf = A2("p2_yf", [128, 16 * P2TT], BF16)
    b_woall = A2("p2_woall", [128, 64 * 128], BF16)
    b_wo = [b_woall[:, 0:32 * 128], b_woall[:, 32 * 128:64 * 128]]
    b_w13 = [A2(f"p2_w13{i}", [128, 16 * 128], BF16) for i in range(4)]
    b_w2s = [A2(f"p2_w2s{i}", [128, 64 * 128], BF16) for i in range(2)]
    b_x2 = [A2(f"p2_x2{i}", [128, P2TT], F32) for i in range(2)]
    b_hid = A2("p2_hid", [128, P2TT], BF16)
    b_hid2 = A2("p2_hid2", [128, P2TT], BF16)
    b_bc1 = A2("p2_bc1", [128, P2TT], F32)
    b_bc2 = A2("p2_bc2", [128, P2TT], F32)
    b_r1 = A2("p2_r1", [1, P2TT], F32)
    b_r2 = A2("p2_r2", [1, P2TT], F32)
    b_r3 = A2("p2_r3", [1, P2TT], F32)
    b_sq2b = A2("p2_sq2b", [128, P2TT], BF16)
    b_x2o = A2("p2_x2o", [128, P2TT], F32)
    b_ln2 = b_woall

    gsl_v = [d_gsl[q].rearrange("o (r t) -> (o r) t", t=T16) for q in range(4)]
    xp2_v = d_xp2.rearrange("p (k t) -> p k t", k=KCH)
    out_v = d_out.rearrange("p (k t) -> p k t", k=KCH)
    wobf_v = d_wobf
    w13_v = d_w13bf
    w2t_rv = d_w2t

    def stats_rows(psum_s1, psum_s2, t1mark, eps_col):
        v1 = EV(lambda v: v.tensor_scalar(b_r1[:], psum_s1, c_small[0:1, 16:17],
                                          None, ALU.mult), wait=[t1mark])
        v2 = EV(lambda v: v.scalar_tensor_tensor(b_r3[:], b_r1[:], -1.0, b_r1[:],
                                                 ALU.mult, ALU.mult), wait=[v1])
        v3 = EV(lambda v: v.scalar_tensor_tensor(b_r3[:], psum_s2, 1.0 / 2048.0,
                                                 b_r3[:], ALU.mult, ALU.add), wait=[v2])
        s1 = ES(lambda s: s.activation(b_r3[:], b_r3[:], AF.Ln, bias=eps_col), wait=[v3])
        s2 = ES(lambda s: s.activation(b_r2[:], b_r3[:], AF.Exp,
                                       scale=c_small[0:1, 14:15]), wait=[s1])
        MM("uso", p_uso[:, 0:P2TT], c_ones_rowF[0:1, 0:128], b_r1[:], wait=[v1],
           inc=[(TE, 1)])
        t_a = (TE, TE.n)
        e1 = EV(lambda v: v.tensor_copy(b_bc1[:], p_uso[:, 0:P2TT]), wait=[t_a])
        consume("uso", e1)
        MM("uso", p_uso[:, 0:P2TT], c_ones_rowF[0:1, 0:128], b_r2[:], wait=[s2],
           inc=[(TE, 1)])
        t_b = (TE, TE.n)
        e2 = EV(lambda v: v.tensor_copy(b_bc2[:], p_uso[:, 0:P2TT]), wait=[t_b])
        consume("uso", e2)
        return (v3, e1, e2)

    sc.mark("p2")
    for i2 in range(cfg.NT2):
        sc.mark(f"p2t{i2}")
        q = (i2 * P2TT) // T16
        agq = ag_q[q]
        c0 = (i2 * P2TT) % T16
        for kc in range(32):
            sc.emit("sync", lambda s, kc=kc, q=q, c0=c0: s.dma_start(
                out=b_g2[:, kc * P2TT:(kc + 1) * P2TT],
                in_=gsl_v[q][kc * 128:(kc + 1) * 128, c0:c0 + P2TT]),
                wait=[agq, barrier[0], cons.get("g2")], inc=[(G2S, 16)])
        glv = (G2S, G2S.n)
        for oc in range(16):
            wb = b_wo[oc % 2]
            sc.emit("sync", lambda s, oc=oc, wb=wb: s.dma_start(
                out=wb[:], in_=wobf_v[:, oc * 4096:(oc + 1) * 4096]),
                wait=[cons.get(f"wo{oc % 2}")], inc=[(WOS[oc % 2], 16)])
            wv = (WOS[oc % 2], WOS[oc % 2].n)
            pa = p_acc[oc % 2]
            reg = f"acc{oc % 2}"
            for kc in range(32):
                MM(reg, pa[:, 0:P2TT], wb[:, kc * 128:(kc + 1) * 128],
                   b_g2[:, kc * P2TT:(kc + 1) * P2TT],
                   start=(kc == 0), stop=(kc == 31),
                   wait=([glv, wv, barrier[0]] if kc == 0 else ()),
                   inc=([(TE, 1)] if kc == 31 else ()))
            mmv = (TE, TE.n)
            cons[f"wo{oc % 2}"] = mmv
            ev = EV(lambda v, pa=pa, oc=oc: v.tensor_copy(
                b_yf[:, oc * P2TT:(oc + 1) * P2TT], pa[:, 0:P2TT]),
                wait=[mmv, barrier[1], cons.get(f"yf{oc}")])
            consume(reg, ev)
            MM("kq", p_kq[0:1, 0:P2TT], c_ones_col[:],
               b_yf[:, oc * P2TT:(oc + 1) * P2TT], start=(oc == 0), stop=(oc == 15),
               wait=[ev], inc=([(TE, 1)] if oc == 15 else ()))
            sv = ES(lambda s, oc=oc: s.activation(
                b_sq2b[:], b_yf[:, oc * P2TT:(oc + 1) * P2TT], AF.Square),
                wait=[ev, barrier[2], cons.get("sq2b")])
            MM("ap", p_ap[0:1, 0:P2TT], c_ones_col[:], b_sq2b[:],
               start=(oc == 0), stop=(oc == 15), wait=[sv],
               inc=([(TE, 1)] if oc == 15 else ()))
            cons["sq2b"] = (TE, TE.n)
        t1m = (TE, TE.n)
        sm = stats_rows(p_kq[0:1, 0:P2TT], p_ap[0:1, 0:P2TT], t1m,
                        c_small[0:1, 15:16])
        consume("kq", sm[0])
        consume("ap", sm[0])
        x1dv = []
        for oc in range(16):
            xs = b_x2[oc % 2]
            sc.emit("sync", lambda s, oc=oc, xs=xs, i2=i2: s.dma_start(
                out=xs[:], in_=xp2_v[:, oc, i2 * P2TT:(i2 + 1) * P2TT]),
                wait=[cons.get(f"x2{oc % 2}")], inc=[(X2S[oc % 2], 16)])
            xv = (X2S[oc % 2], X2S[oc % 2].n)
            v1 = EV(lambda v, oc=oc: v.tensor_tensor(
                b_x1[:, oc * P2TT:(oc + 1) * P2TT],
                b_yf[:, oc * P2TT:(oc + 1) * P2TT], b_bc1[:], ALU.subtract),
                wait=[sm[1], sm[2]])
            v2 = EV(lambda v, oc=oc: v.tensor_tensor(
                b_x1[:, oc * P2TT:(oc + 1) * P2TT],
                b_x1[:, oc * P2TT:(oc + 1) * P2TT], b_bc2[:], ALU.mult))
            v3 = EV(lambda v, oc=oc, xs=xs: v.tensor_tensor(
                b_x1[:, oc * P2TT:(oc + 1) * P2TT],
                b_x1[:, oc * P2TT:(oc + 1) * P2TT], xs[:], ALU.add), wait=[xv])
            cons[f"x2{oc % 2}"] = v3
            cons[f"yf{oc}"] = v1
            x1dv.append(v3)
        for oc in range(16):
            cv = EV(lambda v, oc=oc: v.tensor_copy(
                b_ln2[:, oc * P2TT:(oc + 1) * P2TT],
                b_x1[:, oc * P2TT:(oc + 1) * P2TT]), wait=[x1dv[oc], cons.get(f"ln2{oc}")])
            MM("kq", p_kq[0:1, 0:P2TT], c_ones_col[:],
               b_ln2[:, oc * P2TT:(oc + 1) * P2TT], start=(oc == 0), stop=(oc == 15),
               wait=[cv], inc=([(TE, 1)] if oc == 15 else ()))
            sv = ES(lambda s, oc=oc: s.activation(
                b_sq2b[:], b_ln2[:, oc * P2TT:(oc + 1) * P2TT], AF.Square),
                wait=[cv, cons.get("sq2b")])
            MM("ap", p_ap[0:1, 0:P2TT], c_ones_col[:], b_sq2b[:],
               start=(oc == 0), stop=(oc == 15), wait=[sv],
               inc=([(TE, 1)] if oc == 15 else ()))
            cons["sq2b"] = (TE, TE.n)
        t2m = (TE, TE.n)
        sm2 = stats_rows(p_kq[0:1, 0:P2TT], p_ap[0:1, 0:P2TT], t2m,
                         c_small[0:1, 15:16])
        consume("kq", sm2[0])
        consume("ap", sm2[0])
        for oc in range(16):
            v1 = EV(lambda v, oc=oc: v.tensor_tensor(
                b_x2o[:], b_x1[:, oc * P2TT:(oc + 1) * P2TT], b_bc1[:], ALU.subtract),
                wait=[sm2[1], sm2[2], cons.get("x2o")])
            v2 = EV(lambda v, oc=oc: v.tensor_tensor(
                b_ln2[:, oc * P2TT:(oc + 1) * P2TT], b_x2o[:], b_bc2[:], ALU.mult))
            cons["x2o"] = v2
            cons[f"ln2{oc}"] = v2
        ln2v = (DV, DV.n)
        for hp in range(64):
            wgt = b_w13[(2 * hp) % 4]
            wlt = b_w13[(2 * hp + 1) % 4]
            sc.emit("sync", lambda s, hp=hp, wgt=wgt: s.dma_start(
                out=wgt[:], in_=w13_v[:, hp * 2048:(hp + 1) * 2048]),
                wait=[cons.get(f"w13{(2 * hp) % 4}")], inc=[(W13S[(2 * hp) % 4], 16)])
            wva = (W13S[(2 * hp) % 4], W13S[(2 * hp) % 4].n)
            sc.emit("sync", lambda s, hp=hp, wlt=wlt: s.dma_start(
                out=wlt[:], in_=w13_v[:, (64 + hp) * 2048:(64 + hp + 1) * 2048]),
                wait=[cons.get(f"w13{(2 * hp + 1) % 4}")], inc=[(W13S[(2 * hp + 1) % 4], 16)])
            wvb = (W13S[(2 * hp + 1) % 4], W13S[(2 * hp + 1) % 4].n)
            for kc in range(KCH):
                MM("acc0", p_acc[0][:, 0:P2TT], wgt[:, kc * 128:(kc + 1) * 128],
                   b_ln2[:, kc * P2TT:(kc + 1) * P2TT],
                   start=(kc == 0), stop=(kc == 15),
                   wait=([wva, ln2v] if kc == 0 else ()),
                   inc=([(TE, 1)] if kc == 15 else ()))
            tg = (TE, TE.n)
            cons[f"w13{(2 * hp) % 4}"] = tg
            for kc in range(KCH):
                MM("acc1", p_acc[1][:, 0:P2TT], wlt[:, kc * 128:(kc + 1) * 128],
                   b_ln2[:, kc * P2TT:(kc + 1) * P2TT],
                   start=(kc == 0), stop=(kc == 15),
                   wait=([wvb] if kc == 0 else ()),
                   inc=([(TE, 1)] if kc == 15 else ()))
            tl = (TE, TE.n)
            cons[f"w13{(2 * hp + 1) % 4}"] = tl
            sv = ES(lambda s: s.activation(b_hid[:], p_acc[0][:, 0:P2TT], AF.Silu),
                    wait=[tg, cons.get("hid")])
            consume("acc0", sv)
            hv = EV(lambda v: v.tensor_tensor(b_hid2[:], b_hid[:],
                                              p_acc[1][:, 0:P2TT], ALU.mult),
                    wait=[sv, tl, cons.get("hid2")])
            consume("acc1", hv)
            cons["hid"] = hv
            MM("sq", p_sq[:, 0:P2TT], c_h128[:], b_hid2[:], wait=[hv], inc=[(TE, 1)])
            th = (TE, TE.n)
            cons["hid2"] = th
            mv = EV(lambda v, hp=hp: v.tensor_copy(
                mids_ap(hp), p_sq[:, 0:P2TT]),
                wait=[th, cons.get(f"mid{hp}")])
            consume("sq", mv)
        cons["wo0"] = tl
        cons["wo1"] = tl
        for oc in range(16):
            ws = b_w2s[oc % 2]
            sc.emit("sync", lambda s, oc=oc, ws=ws: s.dma_start(
                out=ws[:], in_=w2t_rv[:, oc * 8192:(oc + 1) * 8192]),
                wait=[cons.get(f"w2s{oc % 2}")], inc=[(W2TS[oc % 2], 16)])
            wv = (W2TS[oc % 2], W2TS[oc % 2].n)
            pa = p_acc[oc % 2]
            reg = f"acc{oc % 2}"
            for kc in range(64):
                MM(reg, pa[:, 0:P2TT], ws[:, kc * 128:(kc + 1) * 128],
                   mids_ap(kc),
                   start=(kc == 0), stop=(kc == 63),
                   wait=([wv, (DV, DV.n)] if kc == 0 else ()),
                   inc=([(TE, 1)] if kc == 63 else ()))
            mmv = (TE, TE.n)
            cons[f"w2s{oc % 2}"] = mmv
            ev = EV(lambda v, pa=pa, oc=oc: v.tensor_copy(
                b_yf[:, oc * P2TT:(oc + 1) * P2TT], pa[:, 0:P2TT]),
                wait=[mmv, cons.get(f"yf{oc}")])
            consume(reg, ev)
            MM("kq", p_kq[0:1, 0:P2TT], c_ones_col[:],
               b_yf[:, oc * P2TT:(oc + 1) * P2TT], start=(oc == 0), stop=(oc == 15),
               wait=[ev], inc=([(TE, 1)] if oc == 15 else ()))
            sv = ES(lambda s, oc=oc: s.activation(
                b_sq2b[:], b_yf[:, oc * P2TT:(oc + 1) * P2TT], AF.Square),
                wait=[ev, cons.get("sq2b")])
            MM("ap", p_ap[0:1, 0:P2TT], c_ones_col[:], b_sq2b[:],
               start=(oc == 0), stop=(oc == 15), wait=[sv],
               inc=([(TE, 1)] if oc == 15 else ()))
            cons["sq2b"] = (TE, TE.n)
        t3m = (TE, TE.n)
        sm3 = stats_rows(p_kq[0:1, 0:P2TT], p_ap[0:1, 0:P2TT], t3m,
                         c_small[0:1, 15:16])
        consume("kq", sm3[0])
        consume("ap", sm3[0])
        for hp in range(64):
            cons[f"mid{hp}"] = t3m
        for oc in range(16):
            v1 = EV(lambda v, oc=oc: v.tensor_tensor(
                b_x2o[:], b_yf[:, oc * P2TT:(oc + 1) * P2TT], b_bc1[:], ALU.subtract),
                wait=[sm3[1], sm3[2], cons.get("x2o")])
            v2 = EV(lambda v: v.tensor_tensor(b_x2o[:], b_x2o[:], b_bc2[:], ALU.mult))
            v3 = EV(lambda v, oc=oc: v.tensor_tensor(
                b_x2o[:], b_x2o[:], b_x1[:, oc * P2TT:(oc + 1) * P2TT], ALU.add))
            cons["x2o"] = v3
            cons[f"yf{oc}"] = v3
            cons[f"ln2{oc}"] = v3
            sc.emit("sync", lambda s, oc=oc, i2=i2: s.dma_start(
                out=out_v[:, oc, i2 * P2TT:(i2 + 1) * P2TT], in_=b_x2o[:]),
                wait=[v3], inc=[(ST, 16)])
            cons["x2o"] = (ST, ST.n)
        cons["g2"] = (TE, TE.n)

    sc.emit("sync", lambda s: s.nop(), wait=[(ST, ST.n)])

    return nc, sc, dict(dbg=dbg)


_CACHE = {}


def kernel(**inputs):
    cfg = Cfg(T=16384, debug=False)
    key = (cfg.T, cfg.debug)
    if key not in _CACHE:
        nc, sc, aux = build(cfg)
        for s in sc.sems.values():
            s.h = nc.alloc_semaphore(s.name)
        with nc.Block() as block:
            sc.replay(block)
        _CACHE[key] = (nc, aux)
    nc, aux = _CACHE[key]
    in_maps = shard_inputs(cfg, inputs)
    res = run_bass_kernel_spmd(nc, in_maps, core_ids=list(range(8)))
    T, T4, T16 = cfg.T, cfg.T4, cfg.T16
    out = np.empty((2, T, 2048), np.float32)
    for c in range(8):
        b, g = c // 4, c % 4
        y = res.results[c]["yout"].reshape(128, 16, 4 * T16)
        y = y.transpose(1, 0, 2).reshape(2048, 4 * T16)
        for q in range(4):
            sl = slice(q * T4 + g * T16, q * T4 + (g + 1) * T16)
            out[b, sl, :] = y[:, q * T16:(q + 1) * T16].T
    return out
